# version 11
# baseline (speedup 1.0000x reference)
import numpy as np
import ml_dtypes
import concourse.bass as bass
import concourse.mybir as mybir
from concourse.bass_utils import run_bass_kernel_spmd
from contextlib import ExitStack

F32 = mybir.dt.float32
BF16 = mybir.dt.bfloat16
AF = mybir.ActivationFunctionType
ALU = mybir.AluOpType
AX = mybir.AxisListType

D = 1024
DI = 2048
DS = 16
E = 8
IN = 1024
SH = 2048
NCORES = 8
NCH = 4
CW = 512
ACC_DVE = (3, 7, 11, 15)


class Sched:
    ENGS = ("sync", "scalar", "vector", "tensor", "gpsimd")

    def __init__(self, nc):
        self.nc = nc
        self.items = {e: [] for e in self.ENGS}
        self.sems = {}
        self.cnt = {}

    def sem(self, name):
        self.sems[name] = self.nc.alloc_semaphore(name)
        self.cnt[name] = 0

    QUEUE_LANES = {"sync": 8, "scalar": 6, "gpsimd": 6}

    def init_lanes(self):
        self.lanes = {}
        self.lane_rr = {}
        self.groups = {}
        for q, n in self.QUEUE_LANES.items():
            self.lanes[q] = []
            for i in range(n):
                name = f"dma_{q}_{i}"
                self.sem(name)
                self.lanes[q].append(name)
            self.lane_rr[q] = 0

    def op(self, eng, fn, waits=(), inc=None, incn=1):
        tok = None
        ws = [w for w in waits if w is not None]
        if inc is not None and incn == 16:
            group = inc
            lanes = self.lanes[eng]
            li = self.lane_rr[eng] % len(lanes)
            self.lane_rr[eng] += 1
            lsem = lanes[li]
            gate = self.cnt[lsem]
            if gate > 0:
                ws.append((lsem, gate))
            self.cnt[lsem] += 16
            tok = (lsem, self.cnt[lsem])
            self.groups.setdefault(group, {})[lsem] = self.cnt[lsem]
            self.items[eng].append((fn, tuple(ws), (lsem, 16)))
            return tok
        if inc is not None:
            self.cnt[inc] += incn
            tok = (inc, self.cnt[inc])
        self.items[eng].append((fn, tuple(ws), (inc, incn) if inc else None))
        return tok

    def group_waits(self, group):
        return [(s, v) for s, v in self.groups.get(group, {}).items()]

    def snap(self):
        return [(s, v) for s, v in self.cnt.items() if v > 0]

    def emit(self, block):
        for e in self.ENGS:
            items = self.items[e]
            if not items:
                continue

            def body(h, items=items):
                lastw = {}
                for fn, ws, inc in items:
                    for (s, v) in ws:
                        if lastw.get(s, 0) < v:
                            h.wait_ge(self.sems[s], v)
                            lastw[s] = v
                    ins = fn(h)
                    if inc is not None:
                        s, n = inc
                        ins.then_inc(self.sems[s], n)
            getattr(block, e)(body)


def build_program(T):
    nc = bass.Bass("TRN2", target_bir_lowering=False, detect_race_conditions=False)
    NT = T // 128
    NTC = T // 512

    def din(name, shape, dt=F32):
        return nc.dram_tensor(name, list(shape), dt, kind="ExternalInput")

    x_h = din("x", (T, D))
    w_in_h = din("w_in", (D, 2 * DI))
    cw_h = din("cw", (128, 16 * 4))
    cb_h = din("cb", (128, 16))
    wx_h = din("w_x", (DI, 2 * DS))
    wdt_h = din("w_dt", (DI, DI))
    bdt_h = din("b_dt", (128, 16))
    wout_h = din("w_out", (DI, D))
    a_h = din("a_mat", (128, 16 * DS))
    dv_h = din("d_vec", (128, 16))
    gwt_h = din("gate_wt", (128, 8 * E), BF16)
    gb_h = din("gate_b", (1, E))
    w1t_h = din("w1t", (E * D, IN))
    b1t_h = din("b1t", (128, 8 * E))
    w3t_h = din("w3t", (E * D, IN))
    b3t_h = din("b3t", (128, 8 * E))
    w2t_h = din("w2t", (E * IN, D))
    b2_h = din("b2", (E, D))
    ws1t_h = din("ws1t", (D, SH))
    bs1_h = din("bs1", (128, 16))
    ws3t_h = din("ws3t", (D, SH))
    bs3_h = din("bs3", (128, 16))
    ws2t_h = din("ws2t", (SH, D))
    bs2_h = din("bs2", (128, 8))
    sel16_h = din("sel16", (16, 16 * 128), BF16)
    sel8_h = din("sel8", (8, 8 * 128))
    identb_h = din("identb", (128, 128), BF16)
    identf_h = din("identf", (128, 128))
    ones1_h = din("ones1", (1, 128))
    fc_h = din("fc", (128, 2))

    out_h = nc.dram_tensor("out", [T, D], F32, kind="ExternalOutput")

    sz_d = nc.dram_tensor("sz_d", [DI, T], BF16, kind="Internal")
    delta_d = nc.dram_tensor("delta_d", [DI, T], F32, kind="Internal")
    u_d = nc.dram_tensor("u_d", [DI, T], BF16, kind="Internal")
    ybf_d = nc.dram_tensor("ybf_d", [DI, T], BF16, kind="Internal")
    x1_d = nc.dram_tensor("x1_d", [T, D], F32, kind="Internal")
    bc_d = nc.dram_tensor("bc_d", [64, T], BF16, kind="Internal")

    S = Sched(nc)
    for s in ("sT", "sV", "sA", "sP"):
        S.sem(s)
    S.init_lanes()

    with ExitStack() as st:
        ec = st.enter_context

        identb = ec(nc.sbuf_tensor("identb_s", [128, 128], BF16))
        identf = ec(nc.sbuf_tensor("identf_s", [128, 128], F32))
        ones1 = ec(nc.sbuf_tensor("ones1_s", [1, 128], F32))
        gbrow = ec(nc.sbuf_tensor("gbrow", [1, E], F32))
        cw_t = ec(nc.sbuf_tensor("cw_t", [128, 16 * 4], F32))
        cb_t = ec(nc.sbuf_tensor("cb_t", [128, 16], F32))
        bdt_t = ec(nc.sbuf_tensor("bdt_t", [128, 16], F32))
        fc = ec(nc.sbuf_tensor("fc_s", [128, 2], F32))

        def cl(dst, src, waits=()):
            return S.op("sync", lambda h, d=dst, s=src: h.dma_start(out=d, in_=s),
                        waits=waits, inc="dX", incn=16)
        cl(identb[:, :], identb_h[:, :])
        cl(identf[:, :], identf_h[:, :])
        cl(ones1[:, :], ones1_h[:, :])
        cl(gbrow[:, :], gb_h[:, :])
        cl(cw_t[:, :], cw_h[:, :])
        cl(cb_t[:, :], cb_h[:, :])
        cl(bdt_t[:, :], bdt_h[:, :])
        cl(fc[:, :], fc_h[:, :])

        u_spill = {}
        sz_spill = {}
        ddma = {}
        bc_spill = {}

        pA = ExitStack()
        big = pA.enter_context(nc.sbuf_tensor("big", [128, 16 * T], BF16))
        pW = ExitStack()
        actfm = pW.enter_context(nc.sbuf_tensor("actfm", [128, 8 * T], BF16))
        wza = pW.enter_context(nc.sbuf_tensor("wza", [128, 8 * DI], BF16))
        wxb = pW.enter_context(nc.sbuf_tensor("wxb", [128, 8 * DI], BF16))

        with ExitStack() as p0:
            pec = p0.enter_context
            x_tm = [pec(nc.sbuf_tensor(f"x_tm{i}", [128, D], F32)) for i in range(2)]
            xibf = [pec(nc.sbuf_tensor(f"xibf{i}", [128, D], BF16)) for i in range(2)]
            ncol = [pec(nc.sbuf_tensor(f"ncol{i}", [128, 1], F32)) for i in range(2)]
            nmean = [pec(nc.sbuf_tensor(f"nmean{i}", [128, 1], F32)) for i in range(2)]
            ptb = [pec(nc.psum_tensor(f"ptb{i}", [128, 128], BF16)) for i in range(2)]

            ptb_free = [None, None]
            act_toks = [None, None]
            for r in range(NT):
                b = r % 2
                ld = S.op("sync",
                          lambda h, b=b, r=r: h.dma_start(out=x_tm[b][:, :], in_=x_h[r * 128:(r + 1) * 128, :]),
                          waits=[act_toks[b]], inc="dX", incn=16)
                t1 = S.op("vector",
                          lambda h, b=b: h.reduce_sum(ncol[b][:, :], x_tm[b][:, :], axis=AX.X, negate=True),
                          waits=[ld], inc="sV")
                t2 = S.op("vector",
                          lambda h, b=b: h.tensor_single_scalar(nmean[b][:, :], ncol[b][:, :], 1.0 / D, ALU.mult),
                          waits=[t1], inc="sV")
                act_toks[b] = S.op("scalar",
                                   lambda h, b=b: h.activation(xibf[b][:, :], x_tm[b][:, :], AF.Identity,
                                                               bias=nmean[b][:, 0:1]),
                                   waits=[t2], inc="sA")
                for dc in range(8):
                    pb = dc % 2
                    tt = S.op("tensor",
                              lambda h, b=b, dc=dc, pb=pb: h.transpose(
                                  ptb[pb][:, :], xibf[b][:, dc * 128:(dc + 1) * 128], identb[:, :]),
                              waits=[act_toks[b], ptb_free[pb]], inc="sT")
                    ptb_free[pb] = S.op(
                        "vector",
                        lambda h, dc=dc, r=r, pb=pb: h.tensor_copy(
                            actfm[:, dc * T + r * 128: dc * T + (r + 1) * 128], ptb[pb][:, :]),
                        waits=[tt], inc="sV")
            p0_done = S.snap()

        with ExitStack() as p1:
            pec = p1.enter_context
            xz = [pec(nc.sbuf_tensor(f"xz{i}", [128, T + 4], BF16)) for i in range(2)]
            diag = [pec(nc.sbuf_tensor(f"diag{i}", [128, 4 * 128], BF16)) for i in range(2)]
            szt = [pec(nc.sbuf_tensor(f"szt{i}", [128, 512], BF16)) for i in range(2)]
            pz = [pec(nc.psum_tensor(f"pz{i}", [128, 512], F32)) for i in range(2)]
            pc = [pec(nc.psum_tensor(f"pc{i}", [128, 512], F32)) for i in range(2)]

            zpad = []
            for i in range(2):
                S.op("vector", lambda h, i=i: h.memset(xz[i][:, 0:1], 0.0), inc="sV")
                zpad.append(S.op("vector", lambda h, i=i: h.memset(xz[i][:, T + 1:T + 4], 0.0), inc="sV"))

            wz_tok = S.op("gpsimd",
                          lambda h: h.dma_start(out=wza[:, :].rearrange("p (k m) -> p k m", m=DI),
                                                in_=w_in_h[:, DI:2 * DI].rearrange("(k p) m -> p k m", p=128)),
                          inc="dW", incn=16)
            wxb_tok = S.op("gpsimd",
                           lambda h: h.dma_start(out=wxb[:, :].rearrange("p (k m) -> p k m", m=DI),
                                                 in_=w_in_h[:, 0:DI].rearrange("(k p) m -> p k m", p=128)),
                           inc="dW", incn=16)

            pz_free = [None, None]
            pc_free = [None, None]
            szt_free = [None, None]

            for mz in range(16):
                for tc in range(NTC):
                    pb = (mz * NTC + tc) % 2
                    sb = pb
                    mt = S.op("tensor",
                              lambda h, mz=mz, tc=tc, pb=pb: [
                                  h.matmul(pz[pb][:, :], wza[:, k * DI + mz * 128:k * DI + (mz + 1) * 128],
                                           actfm[:, k * T + tc * 512:k * T + (tc + 1) * 512],
                                           start=(k == 0), stop=(k == 7)) for k in range(8)][-1],
                              waits=[wz_tok, pz_free[pb]] + (p0_done if mz == 0 and tc == 0 else []),
                              inc="sT")
                    sil = S.op("scalar",
                               lambda h, pb=pb, sb=sb: h.activation(szt[sb][:, :], pz[pb][:, :], AF.Silu),
                               waits=[mt, szt_free[sb]], inc="sA")
                    pz_free[pb] = sil
                    sz_spill[(mz, tc)] = S.op("scalar",
                                              lambda h, sb=sb, mz=mz, tc=tc: h.dma_start(
                                                  out=sz_d[mz * 128:(mz + 1) * 128, tc * 512:(tc + 1) * 512],
                                                  in_=szt[sb][:, :]),
                                              waits=[sil], inc="dSp", incn=16)
                    szt_free[sb] = sz_spill[(mz, tc)]

            xz_free = [None, None]
            dg_free = [None, None]
            u_done = {}
            for mx in range(16):
                b = mx % 2
                ev = []
                for tc in range(NTC):
                    pb = tc % 2
                    mt = S.op("tensor",
                              lambda h, mx=mx, tc=tc, pb=pb: [
                                  h.matmul(pz[pb][:, :], wxb[:, k * DI + mx * 128:k * DI + (mx + 1) * 128],
                                           actfm[:, k * T + tc * 512:k * T + (tc + 1) * 512],
                                           start=(k == 0), stop=(k == 7)) for k in range(8)][-1],
                              waits=[wxb_tok, pz_free[pb]] + ([xz_free[b]] if tc == 0 else []),
                              inc="sT")
                    e = S.op("vector",
                             lambda h, b=b, tc=tc, pb=pb: h.tensor_copy(
                                 xz[b][:, 1 + tc * 512:1 + (tc + 1) * 512], pz[pb][:, :]),
                             waits=[mt], inc="sV")
                    pz_free[pb] = e
                    ev.append(e)
                dg = None
                for k in range(4):
                    dg = S.op("vector",
                              lambda h, b=b, mx=mx, k=k: h.tensor_scalar(
                                  diag[b][:, k * 128:(k + 1) * 128], identb[:, :],
                                  cw_t[:, mx * 4 + k:mx * 4 + k + 1], None, ALU.mult),
                              waits=[dg_free[b]] if k == 0 else [], inc="sV")
                for tc in range(NTC):
                    pb = tc % 2
                    ct = S.op("tensor",
                              lambda h, b=b, tc=tc, pb=pb: [
                                  h.matmul(pc[pb][:, :], diag[b][:, k * 128:(k + 1) * 128],
                                           xz[b][:, tc * 512 + k: tc * 512 + k + 512],
                                           start=(k == 0), stop=(k == 3)) for k in range(4)][-1],
                              waits=[ev[-1], dg, zpad[b], pc_free[pb]], inc="sT")
                    if tc == NTC - 1:
                        xz_free[b] = ct
                        dg_free[b] = ct
                    u_done[(mx, tc)] = S.op("scalar",
                                            lambda h, mx=mx, tc=tc, pb=pb: h.activation(
                                                big[:, mx * T + tc * 512: mx * T + (tc + 1) * 512],
                                                pc[pb][:, :], AF.Silu, bias=cb_t[:, mx:mx + 1]),
                                            waits=[ct], inc="sA")
                    pc_free[pb] = u_done[(mx, tc)]
                u_spill[mx] = S.op("scalar",
                                   lambda h, mx=mx: h.dma_start(
                                       out=u_d[mx * 128:(mx + 1) * 128, :],
                                       in_=big[:, mx * T:(mx + 1) * T]),
                                   waits=[u_done[(mx, tc)] for tc in range(NTC)],
                                   inc="dSp", incn=16)
            m1_done = S.snap()
        pW.close()

        with ExitStack() as p2:
            pec = p2.enter_context
            wdta = pec(nc.sbuf_tensor("wdta", [128, 8 * DI], BF16))
            wdtb = pec(nc.sbuf_tensor("wdtb", [128, 8 * DI], BF16))
            wx_t = pec(nc.sbuf_tensor("wx_t", [128, 16 * 32], BF16))
            bst = [pec(nc.sbuf_tensor(f"bst{i}", [64, 512], BF16)) for i in range(2)]
            tmp1 = pec(nc.sbuf_tensor("tmp1", [128, 512], F32))
            dtt = [pec(nc.sbuf_tensor(f"dtt{i}", [128, 512], F32)) for i in range(2)]
            pd = [pec(nc.psum_tensor(f"pd{i}", [128, 512], F32)) for i in range(2)]
            pbcB = pec(nc.psum_tensor("pbcB", [16, 512], F32))
            pbcC = pec(nc.psum_tensor("pbcC", [16, 512], F32))

            wdtA = S.op("gpsimd",
                        lambda h: h.dma_start(out=wdta[:, :].rearrange("p (k m) -> p k m", m=DI),
                                              in_=wdt_h[0:1024, :].rearrange("(k p) m -> p k m", p=128)),
                        waits=m1_done, inc="dW", incn=16)
            wdtB = S.op("gpsimd",
                        lambda h: h.dma_start(out=wdtb[:, :].rearrange("p (k m) -> p k m", m=DI),
                                              in_=wdt_h[1024:2048, :].rearrange("(k p) m -> p k m", p=128)),
                        inc="dW", incn=16)
            wxT = S.op("gpsimd",
                       lambda h: h.dma_start(out=wx_t[:, :].rearrange("p (k m) -> p k m", m=32),
                                             in_=wx_h[:, :].rearrange("(k p) m -> p k m", p=128)),
                       inc="dW", incn=16)

            pbcB_free = None
            pbcC_free = None
            bst_free = [None, None]
            for tc in range(NTC):
                sb = tc % 2
                mb = S.op("tensor",
                          lambda h, tc=tc: [h.matmul(pbcB[:, :], wx_t[:, k * 32:k * 32 + 16],
                                                     big[:, k * T + tc * 512:k * T + (tc + 1) * 512],
                                                     start=(k == 0), stop=(k == 15)) for k in range(16)][-1],
                          waits=[wxT, pbcB_free] + (m1_done if tc == 0 else []), inc="sT")
                eb = S.op("vector", lambda h, sb=sb: h.tensor_copy(bst[sb][0:16, :], pbcB[:, :]),
                          waits=[mb, bst_free[sb]], inc="sV")
                pbcB_free = eb
                mc = S.op("tensor",
                          lambda h, tc=tc: [h.matmul(pbcC[:, :], wx_t[:, k * 32 + 16:k * 32 + 32],
                                                     big[:, k * T + tc * 512:k * T + (tc + 1) * 512],
                                                     start=(k == 0), stop=(k == 15)) for k in range(16)][-1],
                          waits=[pbcC_free], inc="sT")
                ecv = S.op("vector", lambda h, sb=sb: h.tensor_copy(bst[sb][32:48, :], pbcC[:, :]),
                           waits=[mc, eb], inc="sV")
                pbcC_free = ecv
                bc_spill[tc] = S.op("scalar",
                                    lambda h, sb=sb, tc=tc: h.dma_start(
                                        out=bc_d[:, tc * 512:(tc + 1) * 512], in_=bst[sb][:, :]),
                                    waits=[ecv], inc="dSp", incn=16)
                bst_free[sb] = bc_spill[tc]

            pd_free = [None, None]
            dtt_free = [None, None]
            v1prev = [None]
            for md in range(16):
                for tc in range(NTC):
                    pb = (md * NTC + tc) % 2
                    mt = S.op("tensor",
                              lambda h, md=md, tc=tc, pb=pb: [
                                  h.matmul(pd[pb][:, :],
                                           (wdta if k < 8 else wdtb)[:, (k % 8) * DI + md * 128:(k % 8) * DI + (md + 1) * 128],
                                           big[:, k * T + tc * 512:k * T + (tc + 1) * 512],
                                           start=(k == 0), stop=(k == 15)) for k in range(16)][-1],
                              waits=[wdtA, wdtB, pd_free[pb]], inc="sT")
                    a1 = S.op("scalar",
                              lambda h, md=md, pb=pb: h.activation(tmp1[:, :], pd[pb][:, :], AF.Abs,
                                                                   bias=bdt_t[:, md:md + 1]),
                              waits=[mt, v1prev[0]], inc="sA")
                    a2 = S.op("scalar", lambda h: h.activation(tmp1[:, :], tmp1[:, :], AF.Exp, scale=-1.0),
                              inc="sA")
                    a3 = S.op("scalar", lambda h: h.activation(tmp1[:, :], tmp1[:, :], AF.Ln, bias=1.0),
                              inc="sA")
                    db = pb
                    a4 = S.op("scalar",
                              lambda h, md=md, pb=pb, db=db: h.activation(dtt[db][:, :], pd[pb][:, :], AF.Relu,
                                                                          bias=bdt_t[:, md:md + 1]),
                              waits=[dtt_free[db]], inc="sA")
                    pd_free[pb] = a4
                    v1 = S.op("vector",
                              lambda h, db=db: h.tensor_add(dtt[db][:, :], dtt[db][:, :], tmp1[:, :]),
                              waits=[a4, a3], inc="sV")
                    v1prev[0] = v1
                    ddma[(md, tc)] = S.op("scalar",
                                          lambda h, md=md, tc=tc, db=db: h.dma_start(
                                              out=delta_d[md * 128:(md + 1) * 128, tc * 512:(tc + 1) * 512],
                                              in_=dtt[db][:, :]),
                                          waits=[v1], inc="dSp", incn=16)
                    dtt_free[db] = ddma[(md, tc)]
            m2a_done = S.snap()
        pA.close()
        pa_done = m2a_done

        with ExitStack() as p3:
            pec = p3.enter_context
            ring = pec(nc.sbuf_tensor("ring", [128, 4 * 8192], BF16))
            Bbc = pec(nc.sbuf_tensor("Bbc", [128, 8192], BF16))
            Cbc = pec(nc.sbuf_tensor("Cbc", [128, 8192], BF16))
            brc = [pec(nc.sbuf_tensor(f"brc{i}", [16, 512], BF16)) for i in range(2)]
            crc = [pec(nc.sbuf_tensor(f"crc{i}", [16, 512], BF16)) for i in range(2)]
            bhe = pec(nc.sbuf_tensor("bhe", [128, 8192], BF16))
            oy = pec(nc.sbuf_tensor("oy", [128, 4096], F32))
            x1fm = pec(nc.sbuf_tensor("x1fm", [128, 8 * 512], BF16))
            sel16 = pec(nc.sbuf_tensor("sel16_s", [16, 16 * 128], BF16))
            sel8 = pec(nc.sbuf_tensor("sel8_s", [8, 8 * 128], F32))
            a_t = pec(nc.sbuf_tensor("a_t", [128, 16 * DS], F32))
            dv_t = pec(nc.sbuf_tensor("dv_t", [128, 16], F32))
            hc = pec(nc.sbuf_tensor("hc", [128, 256], F32))
            dtl = [pec(nc.sbuf_tensor(f"dtl{i}", [128, 512], F32)) for i in range(2)]
            dAt = [pec(nc.sbuf_tensor(f"dAt{i}", [128, 512], F32)) for i in range(2)]
            yac = [pec(nc.sbuf_tensor(f"yac{i}", [128, 512], F32)) for i in range(2)]
            dut = [pec(nc.sbuf_tensor(f"dut{i}", [128, 512], BF16)) for i in range(2)]
            dBt = [pec(nc.sbuf_tensor(f"dBt{i}", [128, 512], BF16)) for i in range(2)]
            ht = [pec(nc.sbuf_tensor(f"ht{i}", [128, 512], BF16)) for i in range(2)]
            pt = [pec(nc.sbuf_tensor(f"pt{i}", [128, 512], BF16)) for i in range(2)]
            szs = [pec(nc.sbuf_tensor(f"szs{i}", [128, 512], BF16)) for i in range(2)]
            ybt = [pec(nc.sbuf_tensor(f"ybt{i}", [128, 512], BF16)) for i in range(2)]
            ut = [pec(nc.sbuf_tensor(f"ut{i}", [128, 512], BF16)) for i in range(2)]
            gwt_t = pec(nc.sbuf_tensor("gwt_s", [128, 8 * E], BF16))
            b1_t = pec(nc.sbuf_tensor("b1_s", [128, 8 * E], F32))
            b3_t = pec(nc.sbuf_tensor("b3_s", [128, 8 * E], F32))
            bs1_t = pec(nc.sbuf_tensor("bs1_s", [128, 16], F32))
            bs3_t = pec(nc.sbuf_tensor("bs3_s", [128, 16], F32))
            bs2_t = pec(nc.sbuf_tensor("bs2_s", [128, 8], F32))
            b2_t = pec(nc.sbuf_tensor("b2_s", [8, D], F32))
            gbbc = pec(nc.sbuf_tensor("gbbc", [128, E], F32))
            gT = pec(nc.sbuf_tensor("gT_s", [8, 512], F32))
            Gts = pec(nc.sbuf_tensor("Gts", [128, 8 * 512], BF16))
            xrm = pec(nc.sbuf_tensor("xrm", [128, D], F32))
            otm = pec(nc.sbuf_tensor("otm", [128, D], F32))
            silu_t = [pec(nc.sbuf_tensor(f"silu{i}", [128, 512], BF16)) for i in range(2)]
            lg = pec(nc.sbuf_tensor("lg", [128, E], F32))
            sc_t = pec(nc.sbuf_tensor("sc_t", [128, E], F32))
            sc2_t = pec(nc.sbuf_tensor("sc2_t", [128, E], F32))
            mk = pec(nc.sbuf_tensor("mk", [128, E], F32))
            gtm = pec(nc.sbuf_tensor("gtm", [128, E], F32))
            c_ns = pec(nc.sbuf_tensor("c_ns", [128, 1], F32))
            c_mean = pec(nc.sbuf_tensor("c_mean", [128, 1], F32))
            c_sq = pec(nc.sbuf_tensor("c_sq", [128, 1], F32))
            c_var = pec(nc.sbuf_tensor("c_var", [128, 1], F32))
            c_rstd = pec(nc.sbuf_tensor("c_rstd", [128, 1], F32))
            ph1 = [pec(nc.psum_tensor(f"ph1_{i}", [128, 512], F32)) for i in range(2)]
            ph3 = [pec(nc.psum_tensor(f"ph3_{i}", [128, 512], F32)) for i in range(2)]
            py = [pec(nc.psum_tensor(f"py{i}", [128, 512], F32)) for i in range(2)]
            pf = [pec(nc.psum_tensor(f"pf{i}", [128, 128], F32)) for i in range(2)]

            L = {}
            bct = {}

            c0 = cl(sel16[:, :], sel16_h[:, :], waits=pa_done)
            cl(sel8[:, :], sel8_h[:, :])
            tok_at = cl(a_t[:, :], a_h[:, :])
            tok_dvt = cl(dv_t[:, :], dv_h[:, :])
            ffc = []
            ffc.append(cl(gwt_t[:, :], gwt_h[:, :]))
            ffc.append(cl(b1_t[:, :], b1t_h[:, :]))
            ffc.append(cl(b3_t[:, :], b3t_h[:, :]))
            ffc.append(cl(bs1_t[:, :], bs1_h[:, :]))
            ffc.append(cl(bs3_t[:, :], bs3_h[:, :]))
            ffc.append(cl(bs2_t[:, :], bs2_h[:, :]))
            ffc.append(cl(b2_t[:, :], b2_h[:, :]))
            ffc_done = ffc[-1]
            tok_hc0 = S.op("vector", lambda h: h.memset(hc[:, :], 0.0),
                           waits=pa_done, inc="sV")
            gb1 = S.op("tensor", lambda h: h.matmul(pf[0][:, 0:E], ones1[:, :], gbrow[:, :],
                                                    start=True, stop=True),
                       waits=pa_done, inc="sT")
            gb2 = S.op("vector", lambda h: h.tensor_copy(gbbc[:, :], pf[0][:, 0:E]),
                       waits=[gb1, tok_hc0], inc="sV")
            L[("pf", 0)] = gb2

            ring_free = [pa_done, pa_done, pa_done, pa_done]
            ring_ctr = [0]

            def ring_load(dram_ap, extra_waits=()):
                sl = ring_ctr[0] % 4
                ring_ctr[0] += 1
                w = ring_free[sl]
                ws = (list(w) if isinstance(w, list) else [w]) + list(extra_waits)
                tok = S.op("gpsimd",
                           lambda h, sl=sl, dram_ap=dram_ap: h.dma_start(
                               out=ring[:, sl * 8192:(sl + 1) * 8192].rearrange("p (k m) -> p k m", m=1024),
                               in_=dram_ap.rearrange("(k p) m -> p k m", p=128)),
                           waits=ws, inc="dW", incn=16)
                return sl, tok

            def ring_consumed(sl, tok):
                ring_free[sl] = tok

            ybf_spill = {}
            x1_spill = {}
            ln1_done = {}
            lastB = {}
            lastC = {}

            def emit_bcast(c):
                cb2 = c % 2
                ws0 = [bc_spill[c], L.get(("brc", cb2))] + (pa_done if c < 2 else [])
                brl = S.op("sync",
                           lambda h, c=c, cb2=cb2: h.dma_start(
                               out=brc[cb2][:, :], in_=bc_d[0:16, c * 512:(c + 1) * 512]),
                           waits=ws0, inc="dX", incn=16)
                crl = S.op("sync",
                           lambda h, c=c, cb2=cb2: h.dma_start(
                               out=crc[cb2][:, :], in_=bc_d[32:48, c * 512:(c + 1) * 512]),
                           waits=ws0, inc="dX", incn=16)
                mmlast = None
                for s in range(16):
                    jp = s % 2
                    mmB = S.op("tensor",
                               lambda h, s=s, cb2=cb2, jp=jp: h.matmul(
                                   ph3[jp][:, :], sel16[:, s * 128:(s + 1) * 128],
                                   brc[cb2][0:16, :], start=True, stop=True),
                               waits=[brl, c0, L.get(("ph3", jp))], inc="sT")
                    evB = S.op("scalar",
                               lambda h, s=s, jp=jp: h.activation(Bbc[:, s * 512:(s + 1) * 512],
                                                                  ph3[jp][:, :], AF.Identity),
                               waits=[mmB, lastB.get((c - 1, s))] + (pa_done if c == 0 and s == 0 else []),
                               inc="sA")
                    L[("ph3", jp)] = evB
                    bct[(c, 'B', s)] = evB
                    mmC = S.op("tensor",
                               lambda h, s=s, cb2=cb2, jp=jp: h.matmul(
                                   ph1[jp][:, :], sel16[:, s * 128:(s + 1) * 128],
                                   crc[cb2][:, :], start=True, stop=True),
                               waits=[crl, c0, L.get(("ph1", jp))], inc="sT")
                    evC = S.op("scalar",
                               lambda h, s=s, jp=jp: h.activation(Cbc[:, s * 512:(s + 1) * 512],
                                                                  ph1[jp][:, :], AF.Identity),
                               waits=[mmC, lastC.get((c - 1, s))], inc="sA")
                    L[("ph1", jp)] = evC
                    bct[(c, 'C', s)] = evC
                    mmlast = mmC
                L[("brc", cb2)] = mmlast

            sfree = {}
            hcq = {}

            def emit_scan_iter(c, mx):
                db = mx % 2
                pa0 = pa_done if c == 0 else []
                dld = S.op("sync",
                           lambda h, mx=mx, c=c, db=db: h.dma_start(
                               out=dtl[db][:, :], in_=delta_d[mx * 128:(mx + 1) * 128, c * 512:(c + 1) * 512]),
                           waits=[ddma[(mx, c)], sfree.get(("dt", db))] + pa0, inc="dX", incn=16)
                uld = S.op("sync",
                           lambda h, mx=mx, c=c, db=db: h.dma_start(
                               out=ut[db][:, :], in_=u_d[mx * 128:(mx + 1) * 128, c * 512:(c + 1) * 512]),
                           waits=[u_spill[mx], sfree.get(("ut", db))] + pa0, inc="dX", incn=16)
                sld = S.op("sync",
                           lambda h, mx=mx, c=c, db=db: h.dma_start(
                               out=szs[db][:, :], in_=sz_d[mx * 128:(mx + 1) * 128, c * 512:(c + 1) * 512]),
                           waits=[sz_spill[(mx, c)], sfree.get(("sz", db))] + pa0, inc="dX", incn=16)
                dutk = S.op("vector",
                            lambda h, db=db: h.tensor_tensor(dut[db][:, :], dtl[db][:, :], ut[db][:, :],
                                                             ALU.mult),
                            waits=[dld, uld, sfree.get(("dut", db)), tok_hc0], inc="sV")
                acc = None
                for s in range(16):
                    j = s % 2
                    adA = S.op("scalar",
                               lambda h, mx=mx, s=s, j=j, db=db: h.activation(
                                   dAt[j][:, :], dtl[db][:, :], AF.Exp,
                                   scale=a_t[:, mx * DS + s:mx * DS + s + 1]),
                               waits=[dld, sfree.get(("dA", j)), tok_at], inc="sA")
                    dBm = S.op("vector",
                               lambda h, s=s, j=j, db=db: h.tensor_tensor(
                                   dBt[j][:, :], dut[db][:, :], Bbc[:, s * 512:(s + 1) * 512], ALU.mult),
                               waits=[dutk, sfree.get(("dB", j)), bct[(c, 'B', s)]], inc="sV")
                    if mx == 15:
                        lastB[(c, s)] = dBm
                    hsc = S.op("vector",
                               lambda h, mx=mx, s=s, j=j: h.tensor_tensor_scan(
                                   ht[j][:, :], dAt[j][:, :], dBt[j][:, :],
                                   hc[:, mx * DS + s:mx * DS + s + 1], ALU.mult, ALU.add),
                               waits=[adA, dBm, sfree.get(("h", j)), sfree.get(("h2", j)),
                                      hcq.get((mx, s))], inc="sV")
                    sfree[("dA", j)] = hsc
                    sfree[("dB", j)] = hsc
                    hcw = S.op("scalar",
                               lambda h, mx=mx, s=s, j=j: h.activation(
                                   hc[:, mx * DS + s:mx * DS + s + 1], ht[j][:, 511:512], AF.Identity),
                               waits=[hsc], inc="sA")
                    hcq[(mx, s)] = hcw
                    sfree[("h2", j)] = hcw
                    pm = S.op("vector",
                              lambda h, s=s, j=j: h.tensor_tensor(
                                  pt[j][:, :], ht[j][:, :], Cbc[:, s * 512:(s + 1) * 512], ALU.mult),
                              waits=[hsc, bct[(c, 'C', s)], sfree.get(("pt", j))], inc="sV")
                    sfree[("h", j)] = pm
                    if mx == 15:
                        lastC[(c, s)] = pm
                    if s == 0:
                        acc = S.op("vector",
                                   lambda h, j=j, db=db: h.tensor_copy(yac[db][:, :], pt[j][:, :]),
                                   waits=[pm, sfree.get(("yac", db))], inc="sV")
                    else:
                        acc = S.op("gpsimd",
                                   lambda h, j=j, db=db: h.tensor_tensor(
                                       yac[db][:, :], yac[db][:, :], pt[j][:, :], ALU.add),
                                   waits=[pm, acc], inc="sP")
                    sfree[("pt", j)] = acc
                y1 = S.op("vector",
                          lambda h, mx=mx, db=db: h.scalar_tensor_tensor(
                              yac[db][:, :], ut[db][:, :], dv_t[:, mx:mx + 1], yac[db][:, :],
                              ALU.mult, ALU.add),
                          waits=[acc, uld, tok_dvt], inc="sV")
                sfree[("ut", db)] = y1
                yo = mx % 2
                y2 = S.op("gpsimd",
                          lambda h, yo=yo, db=db: h.tensor_tensor(
                              ybt[yo][:, :], yac[db][:, :], szs[db][:, :], ALU.mult),
                          waits=[y1, sld, sfree.get(("ybt", yo))], inc="sP")
                sfree[("yac", db)] = y2
                sfree[("sz", db)] = y2
                sfree[("dt", db)] = y2
                sfree[("dut", db)] = y2
                ydma = S.op("scalar",
                            lambda h, mx=mx, c=c, yo=yo: h.dma_start(
                                out=ybf_d[mx * 128:(mx + 1) * 128, c * 512:(c + 1) * 512],
                                in_=ybt[yo][:, :]),
                            waits=[y2], inc="dSp", incn=16)
                sfree[("ybt", yo)] = ydma
                ybf_spill[(mx, c)] = ydma

            def emit_ln(r, src, coef, res_spill, do_x1, out_tag):
                rl = r % 4
                xsrc = x_h if out_tag == "ln1" else x1_d
                xw = [sfree.get("xrm")]
                if out_tag == "ln2":
                    xw.append(x1_spill[r])
                xr = S.op("sync",
                          lambda h, r=r, xsrc=xsrc: h.dma_start(out=xrm[:, :],
                                                                in_=xsrc[r * 128:(r + 1) * 128, :]),
                          waits=xw, inc="dX", incn=16)
                ot = None
                for dc in range(8):
                    fb = dc % 2
                    tt = S.op("tensor",
                              lambda h, dc=dc, rl=rl: h.transpose(
                                  pf[dc % 2][:, 0:128],
                                  oy[:, dc * 512 + rl * 128: dc * 512 + (rl + 1) * 128],
                                  identf[:, :]),
                              waits=[L.get(("oyw", dc)), L.get(("pf", fb))], inc="sT")
                    L[("oyrd", out_tag, dc)] = tt
                    ot = S.op("vector",
                              lambda h, dc=dc, fb=fb, coef=coef: h.scalar_tensor_tensor(
                                  otm[:, dc * 128:(dc + 1) * 128], xrm[:, dc * 128:(dc + 1) * 128],
                                  coef, pf[fb][:, 0:128], ALU.mult, ALU.add),
                              waits=[tt, xr, L.get("otm")], inc="sV")
                    L[("pf", fb)] = ot
                s1 = S.op("vector", lambda h: h.reduce_sum(c_ns[:, :], otm[:, :], axis=AX.X, negate=True),
                          waits=[ot], inc="sV")
                s2 = S.op("vector", lambda h: h.tensor_single_scalar(c_mean[:, :], c_ns[:, :], -1.0 / D, ALU.mult),
                          waits=[s1], inc="sV")
                s3 = S.op("scalar", lambda h: h.activation(xrm[:, :], otm[:, :], AF.Square,
                                                           accum_out=c_sq[:, 0:1]),
                          waits=[ot], inc="sA")
                sfree["xrm"] = s3
                s4 = S.op("scalar", lambda h: h.activation(c_var[:, :], c_mean[:, :], AF.Square),
                          waits=[s2], inc="sA")
                s5 = S.op("vector", lambda h: h.tensor_scalar(c_ns[:, :], c_sq[:, :], 1.0 / D,
                                                              c_var[:, 0:1], ALU.mult, ALU.subtract),
                          waits=[s3, s4], inc="sV")
                s6 = S.op("scalar", lambda h: h.activation(c_rstd[:, :], c_ns[:, :], AF.Ln, bias=fc[:, 0:1]),
                          waits=[s5], inc="sA")
                s7 = S.op("scalar", lambda h: h.activation(c_rstd[:, :], c_rstd[:, :], AF.Exp, scale=fc[:, 1:2]),
                          inc="sA")
                ap = S.op("vector",
                          lambda h: h.tensor_scalar(otm[:, :], otm[:, :], c_mean[:, 0:1],
                                                    c_rstd[:, 0:1], ALU.subtract, ALU.mult),
                          waits=[s7, s3], inc="sV")
                if out_tag == "ln1":
                    sp = S.op("sync",
                              lambda h, r=r: h.dma_start(out=x1_d[r * 128:(r + 1) * 128, :], in_=otm[:, :]),
                              waits=[ap], inc="dX", incn=16)
                    x1_spill[r] = sp
                    lastev = None
                    for dc in range(8):
                        bb = dc % 2
                        tb = S.op("tensor",
                                  lambda h, dc=dc, bb=bb: h.transpose(
                                      pf[bb][:, :], otm[:, dc * 128:(dc + 1) * 128], identf[:, :]),
                                  waits=[ap, L.get(("pf", bb))], inc="sT")
                        lastev = S.op("scalar",
                                      lambda h, dc=dc, rl=rl, bb=bb: h.activation(
                                          x1fm[:, dc * 512 + rl * 128:dc * 512 + (rl + 1) * 128],
                                          pf[bb][:, :], AF.Identity),
                                      waits=[tb, L.get("x1fm")], inc="sA")
                        L[("pf", bb)] = lastev
                    L["otm"] = lastev
                    ln1_done[r] = lastev
                else:
                    L["otm"] = ap
                    L["outdma"] = S.op("sync",
                                       lambda h, r=r: h.dma_start(out=out_h[r * 128:(r + 1) * 128, :],
                                                                  in_=otm[:, :]),
                                       waits=[ap], inc="dO", incn=16)

            def gen(c):
                if c > 0:
                    for rl in range(4):
                        emit_ln((c - 1) * 4 + rl, None, 1.0, None, False, "ln2")
                        yield "ln2"
                if c + 1 < NCH:
                    emit_bcast(c + 1)
                yield "bcast"
                wA_sl, wA = ring_load(wout_h[0:1024, :])
                wB_sl, wB = ring_load(wout_h[1024:2048, :])
                ybl = []
                for k in range(16):
                    ybl.append(S.op("sync",
                                    lambda h, k=k, c=c: h.dma_start(
                                        out=bhe[:, k * 512:(k + 1) * 512],
                                        in_=ybf_d[k * 128:(k + 1) * 128, c * 512:(c + 1) * 512]),
                                    waits=[ybf_spill[(k, c)], L.get(("bhe", k))] +
                                          (pa_done if c == 0 else []),
                                    inc="dX", incn=16))
                yield "m3"
                mts = []
                for mo in range(8):
                    pbk = mo % 2
                    mt = S.op("tensor",
                              lambda h, mo=mo, pbk=pbk, wA_sl=wA_sl, wB_sl=wB_sl: [
                                  h.matmul(py[pbk][:, :],
                                           ring[:, (wA_sl if k < 8 else wB_sl) * 8192 +
                                                (k % 8) * 1024 + mo * 128:
                                                (wA_sl if k < 8 else wB_sl) * 8192 +
                                                (k % 8) * 1024 + (mo + 1) * 128],
                                           bhe[:, k * 512:(k + 1) * 512],
                                           start=(k == 0), stop=(k == 15)) for k in range(16)][-1],
                              waits=[wA, wB, L.get(("py", pbk))] + ybl, inc="sT")
                    mts.append(mt)
                    ev = S.op("scalar",
                              lambda h, mo=mo, pbk=pbk: h.activation(
                                  oy[:, mo * 512:(mo + 1) * 512], py[pbk][:, :], AF.Identity),
                              waits=[mt, L.get(("oyrd", "ln2", mo))], inc="sA")
                    L[("py", pbk)] = ev
                    L[("oyw", mo)] = ev
                ring_consumed(wA_sl, mts[-1])
                ring_consumed(wB_sl, mts[-1])
                for k in range(16):
                    L[("bhe", k)] = mts[-1]
                yield "m3b"
                for rl in range(4):
                    emit_ln(c * 4 + rl, None, 2.0, None, True, "ln1")
                    yield "ln1"

                gate_toks = []
                for rl in range(4):
                    r = c * 4 + rl
                    gm = S.op("tensor",
                              lambda h, rl=rl: [
                                  h.matmul(pf[0][:, 0:E], x1fm[:, k * 512 + rl * 128:k * 512 + (rl + 1) * 128],
                                           gwt_t[:, k * E:(k + 1) * E],
                                           start=(k == 0), stop=(k == 7)) for k in range(8)][-1],
                              waits=[ln1_done[r], gb2, ffc_done, L.get(("pf", 0))], inc="sT")
                    g1 = S.op("vector", lambda h: h.tensor_add(lg[:, :], pf[0][:, 0:E], gbbc[:, :]),
                              waits=[gm, L.get("lg")], inc="sV")
                    L[("pf", 0)] = g1
                    g2 = S.op("vector", lambda h: h.reduce_max(c_ns[:, :], lg[:, :], axis=AX.X),
                              waits=[g1], inc="sV")
                    g2b = S.op("vector", lambda h: h.tensor_single_scalar(c_mean[:, :], c_ns[:, :], -1.0, ALU.mult),
                               waits=[g2], inc="sV")
                    g3 = S.op("scalar", lambda h: h.activation(sc_t[:, :], lg[:, :], AF.Exp, bias=c_mean[:, 0:1]),
                              waits=[g2b, L.get("sc_t")], inc="sA")
                    g4 = S.op("vector", lambda h: h.reduce_sum(c_sq[:, :], sc_t[:, :], axis=AX.X),
                              waits=[g3], inc="sV")
                    g5 = S.op("vector", lambda h: h.reciprocal(c_var[:, :], c_sq[:, :]), waits=[g4], inc="sV")
                    g6 = S.op("vector", lambda h: h.tensor_single_scalar(sc_t[:, :], sc_t[:, :],
                                                                         c_var[:, 0:1], ALU.mult), waits=[g5], inc="sV")
                    g7 = S.op("vector", lambda h: h.reduce_max(c_ns[:, :], sc_t[:, :], axis=AX.X),
                              waits=[g6], inc="sV")
                    g8 = S.op("vector", lambda h: h.tensor_single_scalar(mk[:, :], sc_t[:, :], c_ns[:, 0:1],
                                                                         ALU.is_ge), waits=[g7], inc="sV")
                    g9 = S.op("vector", lambda h: h.scalar_tensor_tensor(sc2_t[:, :], mk[:, :], -1e30, sc_t[:, :],
                                                                         ALU.mult, ALU.add), waits=[g8], inc="sV")
                    g10 = S.op("vector", lambda h: h.reduce_max(c_ns[:, :], sc2_t[:, :], axis=AX.X),
                               waits=[g9], inc="sV")
                    g11 = S.op("vector", lambda h: h.tensor_single_scalar(mk[:, :], sc_t[:, :], c_ns[:, 0:1],
                                                                          ALU.is_ge), waits=[g10], inc="sV")
                    g12 = S.op("vector", lambda h: h.tensor_mul(gtm[:, :], sc_t[:, :], mk[:, :]),
                               waits=[g11], inc="sV")
                    L["lg"] = g12
                    L["sc_t"] = g12
                    tg = S.op("tensor", lambda h: h.transpose(pf[1][0:8, 0:128], gtm[:, :], identf[:, :]),
                              waits=[g12, L.get(("pf", 1))], inc="sT")
                    gev = S.op("vector",
                               lambda h, rl=rl: h.tensor_copy(gT[0:8, rl * 128:(rl + 1) * 128],
                                                              pf[1][0:8, 0:128]),
                               waits=[tg, L.get("gtm")], inc="sV")
                    L[("pf", 1)] = gev
                    L["gtm"] = gev
                    gate_toks.append(gev)
                gate_done = gate_toks[-1]
                gts_tok = {}
                for e in range(E):
                    gmm = S.op("tensor",
                               lambda h, e=e: h.matmul(
                                   py[0][:, :], sel8[:, e * 128:(e + 1) * 128],
                                   gT[:, :], start=True, stop=True),
                               waits=[gate_done, ffc_done, L.get(("py", 0))], inc="sT")
                    gts_tok[e] = S.op("scalar",
                                      lambda h, e=e: h.activation(
                                          Gts[:, e * 512:(e + 1) * 512], py[0][:, :], AF.Identity),
                                      waits=[gmm, L.get(("gts", e))], inc="sA")
                    L[("py", 0)] = gts_tok[e]
                yield "gates"

                experts = []
                for e in range(E):
                    experts.append(dict(
                        w1=w1t_h[e * D:(e + 1) * D, :], w3=w3t_h[e * D:(e + 1) * D, :],
                        w2=w2t_h[e * IN:(e + 1) * IN, :],
                        b1c=[b1_t[:, e * 8 + mi:e * 8 + mi + 1] for mi in range(8)],
                        b3c=[b3_t[:, e * 8 + mi:e * 8 + mi + 1] for mi in range(8)],
                        gated=e))
                experts.append(dict(
                    w1=ws1t_h[:, 0:1024], w3=ws3t_h[:, 0:1024], w2=ws2t_h[0:1024, :],
                    b1c=[bs1_t[:, mi:mi + 1] for mi in range(8)],
                    b3c=[bs3_t[:, mi:mi + 1] for mi in range(8)], gated=None))
                experts.append(dict(
                    w1=ws1t_h[:, 1024:2048], w3=ws3t_h[:, 1024:2048], w2=ws2t_h[1024:2048, :],
                    b1c=[bs1_t[:, 8 + mi:8 + mi + 1] for mi in range(8)],
                    b3c=[bs3_t[:, 8 + mi:8 + mi + 1] for mi in range(8)], gated=None))

                last_x1_rd = None
                for p in range(5):
                    e0, e1 = experts[2 * p], experts[2 * p + 1]
                    sl1a, t1a = ring_load(e0["w1"])
                    sl3a, t3a = ring_load(e0["w3"])
                    hlast = {}
                    for half, ex in ((0, e0), (1, e1)):
                        if half == 0:
                            sl1, t1 = sl1a, t1a
                            sl3, t3 = sl3a, t3a
                        else:
                            sl1, t1 = ring_load(ex["w1"])
                            sl3, t3 = ring_load(ex["w3"])
                        gated = ex["gated"]
                        m1last = None
                        m3last = None
                        for mi in range(8):
                            jp = mi % 2
                            m1_ = S.op("tensor",
                                       lambda h, sl1=sl1, mi=mi, jp=jp: [
                                           h.matmul(ph1[jp][:, :],
                                                    ring[:, sl1 * 8192 + k * 1024 + mi * 128:
                                                         sl1 * 8192 + k * 1024 + (mi + 1) * 128],
                                                    x1fm[:, k * 512:(k + 1) * 512],
                                                    start=(k == 0), stop=(k == 7)) for k in range(8)][-1],
                                       waits=[t1, L.get(("ph1", jp))], inc="sT")
                            m1last = m1_
                            sb = mi % 2
                            sa = S.op("scalar",
                                      lambda h, mi=mi, sb=sb, ex=ex, jp=jp: h.activation(
                                          silu_t[sb][:, :], ph1[jp][:, :], AF.Silu, bias=ex["b1c"][mi]),
                                      waits=[m1_, L.get(("silu", sb))], inc="sA")
                            L[("ph1", jp)] = sa
                            m3_ = S.op("tensor",
                                       lambda h, sl3=sl3, mi=mi, jp=jp: [
                                           h.matmul(ph3[jp][:, :],
                                                    ring[:, sl3 * 8192 + k * 1024 + mi * 128:
                                                         sl3 * 8192 + k * 1024 + (mi + 1) * 128],
                                                    x1fm[:, k * 512:(k + 1) * 512],
                                                    start=(k == 0), stop=(k == 7)) for k in range(8)][-1],
                                       waits=[t3, L.get(("ph3", jp))], inc="sT")
                            m3last = m3_
                            hv = S.op("vector",
                                      lambda h, mi=mi, half=half, ex=ex, sb=sb, jp=jp: h.scalar_tensor_tensor(
                                          bhe[:, half * 4096 + mi * 512:half * 4096 + (mi + 1) * 512],
                                          ph3[jp][:, :], ex["b3c"][mi], silu_t[sb][:, :], ALU.add, ALU.mult),
                                      waits=[m3_, sa, L.get(("bhe", 8 * half + mi))], inc="sV")
                            L[("ph3", jp)] = hv
                            L[("silu", sb)] = hv
                            if gated is not None:
                                hv = S.op("gpsimd",
                                          lambda h, mi=mi, half=half, gated=gated: h.tensor_tensor(
                                              bhe[:, half * 4096 + mi * 512:half * 4096 + (mi + 1) * 512],
                                              bhe[:, half * 4096 + mi * 512:half * 4096 + (mi + 1) * 512],
                                              Gts[:, gated * 512:(gated + 1) * 512], ALU.mult),
                                          waits=[hv, gts_tok[gated]], inc="sP")
                            hlast[8 * half + mi] = hv
                        if gated is not None:
                            L[("gts", gated)] = hlast[8 * half + 7]
                        ring_consumed(sl1, m1last)
                        ring_consumed(sl3, m3last)
                        last_x1_rd = m3last
                        yield "he"
                    sl2a, t2a = ring_load(e0["w2"])
                    sl2b, t2b = ring_load(e1["w2"])
                    wmts = []
                    for mo in range(8):
                        pbk = mo % 2

                        def w2mm(h, sl2a=sl2a, sl2b=sl2b, mo=mo, pbk=pbk, p=p):
                            last = None
                            if p == 0:
                                last = h.matmul(py[pbk][:, :], b2_t[:, mo * 128:(mo + 1) * 128],
                                                gT[:, :], start=True, stop=False)
                            for k in range(16):
                                sl = sl2a if k < 8 else sl2b
                                last = h.matmul(py[pbk][:, :],
                                                ring[:, sl * 8192 + (k % 8) * 1024 + mo * 128:
                                                     sl * 8192 + (k % 8) * 1024 + (mo + 1) * 128],
                                                bhe[:, k * 512:(k + 1) * 512],
                                                start=(p != 0 and k == 0), stop=(k == 15))
                            return last
                        mt = S.op("tensor", w2mm,
                                  waits=[t2a, t2b, L.get(("py", pbk)), gate_done] +
                                        [hlast[k] for k in range(16)], inc="sT")
                        wmts.append(mt)
                        if p == 0:
                            ev = S.op("vector",
                                      lambda h, mo=mo, pbk=pbk: h.tensor_scalar(
                                          oy[:, mo * 512:(mo + 1) * 512], py[pbk][:, :],
                                          bs2_t[:, mo:mo + 1], None, ALU.add),
                                      waits=[mt, L.get(("oyrd", "ln1", mo)), L.get(("oyw2", mo))], inc="sV")
                        else:
                            ev = S.op("vector",
                                      lambda h, mo=mo, pbk=pbk: h.tensor_tensor(
                                          oy[:, mo * 512:(mo + 1) * 512],
                                          oy[:, mo * 512:(mo + 1) * 512], py[pbk][:, :], ALU.add),
                                      waits=[mt, L.get(("oyw2", mo))], inc="sV")
                        L[("py", pbk)] = ev
                        L[("oyw2", mo)] = ev
                        if p == 4:
                            L[("oyw", mo)] = ev
                    ring_consumed(sl2a, wmts[-1])
                    ring_consumed(sl2b, wmts[-1])
                    for k in range(16):
                        L[("bhe", k)] = wmts[-1]
                    yield "pair"
                L["x1fm"] = last_x1_rd

            def pump_until(g, stop_tags):
                while True:
                    try:
                        tag = next(g)
                    except StopIteration:
                        return None
                    if tag in stop_tags:
                        return tag

            emit_bcast(0)
            g_prev = None
            for c in range(NCH):
                if g_prev is not None:
                    if pump_until(g_prev, ("bcast",)) is None:
                        g_prev = None
                for mx in range(16):
                    emit_scan_iter(c, mx)
                    if g_prev is not None:
                        if mx == 0:
                            if pump_until(g_prev, ("gates",)) is None:
                                g_prev = None
                        else:
                            try:
                                next(g_prev)
                            except StopIteration:
                                g_prev = None
                if g_prev is not None:
                    for _ in g_prev:
                        pass
                g_prev = gen(c)
            for _ in g_prev:
                pass
            for rl in range(4):
                emit_ln((NCH - 1) * 4 + rl, None, 1.0, None, False, "ln2")

            S.op("sync", lambda h: h.nop(), waits=S.group_waits("dO"))

        with nc.Block() as block:
            S.emit(block)

    return nc


def _host_layout(ins):
    f = np.float32
    bb = ml_dtypes.bfloat16

    def tile_col(v, nk):
        v = np.asarray(v, f).reshape(nk, 128, -1).transpose(1, 0, 2)
        return np.ascontiguousarray(v.reshape(128, -1))

    w = {}
    w["w_in"] = np.ascontiguousarray(np.asarray(ins["L0_attn_W_in"], f))
    w["cw"] = tile_col(np.asarray(ins["L0_attn_conv_w"], f).reshape(4, DI).T, 16)
    w["cb"] = tile_col(np.asarray(ins["L0_attn_conv_b"], f).reshape(DI, 1), 16)
    w["w_x"] = np.ascontiguousarray(np.asarray(ins["L0_attn_W_x"], f))
    w["w_dt"] = np.ascontiguousarray(np.asarray(ins["L0_attn_W_dt"], f))
    w["b_dt"] = tile_col(np.asarray(ins["L0_attn_b_dt"], f).reshape(DI, 1), 16)
    w["w_out"] = np.ascontiguousarray(np.asarray(ins["L0_attn_W_out"], f))
    w["a_mat"] = tile_col(np.asarray(ins["L0_attn_A"], f), 16)
    w["d_vec"] = tile_col(np.asarray(ins["L0_attn_D"], f).reshape(DI, 1), 16)
    w["gate_wt"] = tile_col(np.ascontiguousarray(np.asarray(ins["L0_ffn_gate_w"], f).T), 8).astype(bb)
    w["gate_b"] = np.asarray(ins["L0_ffn_gate_b"], f).reshape(1, E)
    w["w1t"] = np.ascontiguousarray(np.asarray(ins["L0_ffn_w1"], f).transpose(0, 2, 1).reshape(E * D, IN))
    w["w3t"] = np.ascontiguousarray(np.asarray(ins["L0_ffn_w3"], f).transpose(0, 2, 1).reshape(E * D, IN))
    w["w2t"] = np.ascontiguousarray(np.asarray(ins["L0_ffn_w2"], f).transpose(0, 2, 1).reshape(E * IN, D))
    b1 = np.asarray(ins["L0_ffn_b1"], f)
    b3 = np.asarray(ins["L0_ffn_b3"], f)
    bt1 = np.zeros((128, 8 * E), f)
    bt3 = np.zeros((128, 8 * E), f)
    for e in range(E):
        for mi in range(8):
            bt1[:, e * 8 + mi] = b1[e, mi * 128:(mi + 1) * 128]
            bt3[:, e * 8 + mi] = b3[e, mi * 128:(mi + 1) * 128]
    w["b1t"] = bt1
    w["b3t"] = bt3
    w["b2"] = np.ascontiguousarray(np.asarray(ins["L0_ffn_b2"], f))
    w["ws1t"] = np.ascontiguousarray(np.asarray(ins["L0_ffn_ws1"], f).T)
    w["bs1"] = tile_col(np.asarray(ins["L0_ffn_bs1"], f).reshape(SH, 1), 16)
    w["ws3t"] = np.ascontiguousarray(np.asarray(ins["L0_ffn_ws3"], f).T)
    w["bs3"] = tile_col(np.asarray(ins["L0_ffn_bs3"], f).reshape(SH, 1), 16)
    w["ws2t"] = np.ascontiguousarray(np.asarray(ins["L0_ffn_ws2"], f).T)
    w["bs2"] = tile_col(np.asarray(ins["L0_ffn_bs2"], f).reshape(D, 1), 8)
    sel16 = np.zeros((16, 16 * 128), bb)
    for s in range(16):
        sel16[s, s * 128:(s + 1) * 128] = 1
    w["sel16"] = sel16
    sel8 = np.zeros((8, 8 * 128), np.float32)
    for s in range(8):
        sel8[s, s * 128:(s + 1) * 128] = 1
    w["sel8"] = sel8
    w["identb"] = np.eye(128, dtype=bb)
    w["identf"] = np.eye(128, dtype=f)
    w["ones1"] = np.ones((1, 128), f)
    fcv = np.zeros((128, 2), f)
    fcv[:, 0] = 1e-5
    fcv[:, 1] = -0.5
    w["fc"] = fcv
    return w


def kernel(**ins):
    T = ins["L0_attn_inp"].shape[1]
    w = _host_layout(ins)
    x = np.asarray(ins["L0_attn_inp"], np.float32)
    in_maps = []
    for b in range(NCORES):
        m = dict(w)
        m["x"] = np.ascontiguousarray(x[b])
        in_maps.append(m)
    nc = build_program(T)
    res = run_bass_kernel_spmd(nc, in_maps, core_ids=list(range(NCORES)))
    return np.stack([np.asarray(r["out"]) for r in res.results], axis=0)


# version 12
# speedup vs baseline: 1.2330x; 1.2330x over previous
import numpy as np
import ml_dtypes
import concourse.bass as bass
import concourse.mybir as mybir
from concourse.bass_utils import run_bass_kernel_spmd
from contextlib import ExitStack

F32 = mybir.dt.float32
BF16 = mybir.dt.bfloat16
AF = mybir.ActivationFunctionType
ALU = mybir.AluOpType
AX = mybir.AxisListType

D = 1024
DI = 2048
DS = 16
E = 8
IN = 1024
SH = 2048
NCORES = 8
NCH = 4
CW = 512
ACC_DVE = (3, 7, 11, 15)


class Sched:
    ENGS = ("sync", "scalar", "vector", "tensor", "gpsimd")

    def __init__(self, nc):
        self.nc = nc
        self.items = {e: [] for e in self.ENGS}
        self.sems = {}
        self.cnt = {}

    def sem(self, name):
        self.sems[name] = self.nc.alloc_semaphore(name)
        self.cnt[name] = 0

    QUEUE_LANES = {"sync": 8, "scalar": 6, "gpsimd": 6}

    def init_lanes(self):
        self.lanes = {}
        self.lane_rr = {}
        self.groups = {}
        for q, n in self.QUEUE_LANES.items():
            self.lanes[q] = []
            for i in range(n):
                name = f"dma_{q}_{i}"
                self.sem(name)
                self.lanes[q].append(name)

    def op(self, eng, fn, waits=(), inc=None, incn=1, lanes=None):
        tok = None
        ws = [w for w in waits if w is not None]
        if inc is not None and incn == 16:
            group = inc
            pool = list(lanes) if lanes is not None else list(range(len(self.lanes[eng])))
            rk = (eng, tuple(pool))
            self.lane_rr.setdefault(rk, 0)
            li = pool[self.lane_rr[rk] % len(pool)]
            self.lane_rr[rk] += 1
            lsem = self.lanes[eng][li]
            gate = self.cnt[lsem]
            if gate > 0:
                ws.append((lsem, gate))
            self.cnt[lsem] += 16
            tok = (lsem, self.cnt[lsem])
            self.groups.setdefault(group, {})[lsem] = self.cnt[lsem]
            self.items[eng].append((fn, tuple(ws), (lsem, 16)))
            return tok
        if inc is not None:
            self.cnt[inc] += incn
            tok = (inc, self.cnt[inc])
        self.items[eng].append((fn, tuple(ws), (inc, incn) if inc else None))
        return tok

    def group_waits(self, group):
        return [(s, v) for s, v in self.groups.get(group, {}).items()]

    def snap(self):
        return [(s, v) for s, v in self.cnt.items() if v > 0]

    def emit(self, block):
        for e in self.ENGS:
            items = self.items[e]
            if not items:
                continue

            def body(h, items=items):
                lastw = {}
                for fn, ws, inc in items:
                    for (s, v) in ws:
                        if lastw.get(s, 0) < v:
                            h.wait_ge(self.sems[s], v)
                            lastw[s] = v
                    ins = fn(h)
                    if inc is not None:
                        s, n = inc
                        ins.then_inc(self.sems[s], n)
            getattr(block, e)(body)


def build_program(T):
    nc = bass.Bass("TRN2", target_bir_lowering=False, detect_race_conditions=False)
    NT = T // 128
    NTC = T // 512

    def din(name, shape, dt=F32):
        return nc.dram_tensor(name, list(shape), dt, kind="ExternalInput")

    x_h = din("x", (T, D))
    w_in_h = din("w_in", (D, 2 * DI))
    cw_h = din("cw", (128, 16 * 4))
    cb_h = din("cb", (128, 16))
    wx_h = din("w_x", (DI, 2 * DS))
    wdt_h = din("w_dt", (DI, DI))
    bdt_h = din("b_dt", (128, 16))
    wout_h = din("w_out", (DI, D))
    a_h = din("a_mat", (128, 16 * DS))
    dv_h = din("d_vec", (128, 16))
    gwt_h = din("gate_wt", (128, 8 * E), BF16)
    gb_h = din("gate_b", (1, E))
    w1t_h = din("w1t", (E * D, IN))
    b1t_h = din("b1t", (128, 8 * E))
    w3t_h = din("w3t", (E * D, IN))
    b3t_h = din("b3t", (128, 8 * E))
    w2t_h = din("w2t", (E * IN, D))
    b2_h = din("b2", (E, D))
    ws1t_h = din("ws1t", (D, SH))
    bs1_h = din("bs1", (128, 16))
    ws3t_h = din("ws3t", (D, SH))
    bs3_h = din("bs3", (128, 16))
    ws2t_h = din("ws2t", (SH, D))
    bs2_h = din("bs2", (128, 8))
    sel16_h = din("sel16", (16, 16 * 128), BF16)
    sel8_h = din("sel8", (8, 8 * 128))
    identb_h = din("identb", (128, 128), BF16)
    identf_h = din("identf", (128, 128))
    ones1_h = din("ones1", (1, 128))
    fc_h = din("fc", (128, 2))

    out_h = nc.dram_tensor("out", [T, D], F32, kind="ExternalOutput")

    sz_d = nc.dram_tensor("sz_d", [DI, T], BF16, kind="Internal")
    delta_d = nc.dram_tensor("delta_d", [DI, T], F32, kind="Internal")
    u_d = nc.dram_tensor("u_d", [DI, T], BF16, kind="Internal")
    ybf_d = nc.dram_tensor("ybf_d", [DI, T], BF16, kind="Internal")
    x1_d = nc.dram_tensor("x1_d", [T, D], F32, kind="Internal")
    bc_d = nc.dram_tensor("bc_d", [64, T], BF16, kind="Internal")

    S = Sched(nc)
    for s in ("sT", "sV", "sA", "sP"):
        S.sem(s)
    S.init_lanes()

    with ExitStack() as st:
        ec = st.enter_context

        identb = ec(nc.sbuf_tensor("identb_s", [128, 128], BF16))
        identf = ec(nc.sbuf_tensor("identf_s", [128, 128], F32))
        ones1 = ec(nc.sbuf_tensor("ones1_s", [1, 128], F32))
        gbrow = ec(nc.sbuf_tensor("gbrow", [1, E], F32))
        cw_t = ec(nc.sbuf_tensor("cw_t", [128, 16 * 4], F32))
        cb_t = ec(nc.sbuf_tensor("cb_t", [128, 16], F32))
        bdt_t = ec(nc.sbuf_tensor("bdt_t", [128, 16], F32))
        fc = ec(nc.sbuf_tensor("fc_s", [128, 2], F32))

        def cl(dst, src, waits=()):
            return S.op("sync", lambda h, d=dst, s=src: h.dma_start(out=d, in_=s),
                        waits=waits, inc="dX", incn=16, lanes=(0, 1, 2, 3, 4))
        cl(identb[:, :], identb_h[:, :])
        cl(identf[:, :], identf_h[:, :])
        cl(ones1[:, :], ones1_h[:, :])
        cl(gbrow[:, :], gb_h[:, :])
        cl(cw_t[:, :], cw_h[:, :])
        cl(cb_t[:, :], cb_h[:, :])
        cl(bdt_t[:, :], bdt_h[:, :])
        cl(fc[:, :], fc_h[:, :])

        u_spill = {}
        sz_spill = {}
        ddma = {}
        bc_spill = {}

        pA = ExitStack()
        big = pA.enter_context(nc.sbuf_tensor("big", [128, 16 * T], BF16))
        pW = ExitStack()
        actfm = pW.enter_context(nc.sbuf_tensor("actfm", [128, 8 * T], BF16))
        wza = pW.enter_context(nc.sbuf_tensor("wza", [128, 8 * DI], BF16))
        wxb = pW.enter_context(nc.sbuf_tensor("wxb", [128, 8 * DI], BF16))

        with ExitStack() as p0:
            pec = p0.enter_context
            x_tm = [pec(nc.sbuf_tensor(f"x_tm{i}", [128, D], F32)) for i in range(2)]
            xibf = [pec(nc.sbuf_tensor(f"xibf{i}", [128, D], BF16)) for i in range(2)]
            ncol = [pec(nc.sbuf_tensor(f"ncol{i}", [128, 1], F32)) for i in range(2)]
            nmean = [pec(nc.sbuf_tensor(f"nmean{i}", [128, 1], F32)) for i in range(2)]
            ptb = [pec(nc.psum_tensor(f"ptb{i}", [128, 128], BF16)) for i in range(2)]

            ptb_free = [None, None]
            act_toks = [None, None]
            for r in range(NT):
                b = r % 2
                ld = S.op("sync",
                          lambda h, b=b, r=r: h.dma_start(out=x_tm[b][:, :], in_=x_h[r * 128:(r + 1) * 128, :]),
                          waits=[act_toks[b]], inc="dX", incn=16, lanes=(0, 1, 2, 3, 4))
                t1 = S.op("vector",
                          lambda h, b=b: h.reduce_sum(ncol[b][:, :], x_tm[b][:, :], axis=AX.X, negate=True),
                          waits=[ld], inc="sV")
                t2 = S.op("vector",
                          lambda h, b=b: h.tensor_single_scalar(nmean[b][:, :], ncol[b][:, :], 1.0 / D, ALU.mult),
                          waits=[t1], inc="sV")
                act_toks[b] = S.op("scalar",
                                   lambda h, b=b: h.activation(xibf[b][:, :], x_tm[b][:, :], AF.Identity,
                                                               bias=nmean[b][:, 0:1]),
                                   waits=[t2], inc="sA")
                for dc in range(8):
                    pb = dc % 2
                    tt = S.op("tensor",
                              lambda h, b=b, dc=dc, pb=pb: h.transpose(
                                  ptb[pb][:, :], xibf[b][:, dc * 128:(dc + 1) * 128], identb[:, :]),
                              waits=[act_toks[b], ptb_free[pb]], inc="sT")
                    ptb_free[pb] = S.op(
                        "vector",
                        lambda h, dc=dc, r=r, pb=pb: h.tensor_copy(
                            actfm[:, dc * T + r * 128: dc * T + (r + 1) * 128], ptb[pb][:, :]),
                        waits=[tt], inc="sV")
            p0_done = S.snap()

        with ExitStack() as p1:
            pec = p1.enter_context
            xz = [pec(nc.sbuf_tensor(f"xz{i}", [128, T + 4], BF16)) for i in range(2)]
            diag = [pec(nc.sbuf_tensor(f"diag{i}", [128, 4 * 128], BF16)) for i in range(2)]
            szt = [pec(nc.sbuf_tensor(f"szt{i}", [128, 512], BF16)) for i in range(2)]
            pz = [pec(nc.psum_tensor(f"pz{i}", [128, 512], F32)) for i in range(2)]
            pc = [pec(nc.psum_tensor(f"pc{i}", [128, 512], F32)) for i in range(2)]

            zpad = []
            for i in range(2):
                S.op("vector", lambda h, i=i: h.memset(xz[i][:, 0:1], 0.0), inc="sV")
                zpad.append(S.op("vector", lambda h, i=i: h.memset(xz[i][:, T + 1:T + 4], 0.0), inc="sV"))

            wz_tok = S.op("gpsimd",
                          lambda h: h.dma_start(out=wza[:, :].rearrange("p (k m) -> p k m", m=DI),
                                                in_=w_in_h[:, DI:2 * DI].rearrange("(k p) m -> p k m", p=128)),
                          inc="dW", incn=16)
            wxb_tok = S.op("gpsimd",
                           lambda h: h.dma_start(out=wxb[:, :].rearrange("p (k m) -> p k m", m=DI),
                                                 in_=w_in_h[:, 0:DI].rearrange("(k p) m -> p k m", p=128)),
                           inc="dW", incn=16)

            pz_free = [None, None]
            pc_free = [None, None]
            szt_free = [None, None]

            for mz in range(16):
                for tc in range(NTC):
                    pb = (mz * NTC + tc) % 2
                    sb = pb
                    mt = S.op("tensor",
                              lambda h, mz=mz, tc=tc, pb=pb: [
                                  h.matmul(pz[pb][:, :], wza[:, k * DI + mz * 128:k * DI + (mz + 1) * 128],
                                           actfm[:, k * T + tc * 512:k * T + (tc + 1) * 512],
                                           start=(k == 0), stop=(k == 7)) for k in range(8)][-1],
                              waits=[wz_tok, pz_free[pb]] + (p0_done if mz == 0 and tc == 0 else []),
                              inc="sT")
                    sil = S.op("scalar",
                               lambda h, pb=pb, sb=sb: h.activation(szt[sb][:, :], pz[pb][:, :], AF.Silu),
                               waits=[mt, szt_free[sb]], inc="sA")
                    pz_free[pb] = sil
                    sz_spill[(mz, tc)] = S.op("scalar",
                                              lambda h, sb=sb, mz=mz, tc=tc: h.dma_start(
                                                  out=sz_d[mz * 128:(mz + 1) * 128, tc * 512:(tc + 1) * 512],
                                                  in_=szt[sb][:, :]),
                                              waits=[sil], inc="dSp", incn=16)
                    szt_free[sb] = sz_spill[(mz, tc)]

            xz_free = [None, None]
            dg_free = [None, None]
            u_done = {}
            for mx in range(16):
                b = mx % 2
                ev = []
                for tc in range(NTC):
                    pb = tc % 2
                    mt = S.op("tensor",
                              lambda h, mx=mx, tc=tc, pb=pb: [
                                  h.matmul(pz[pb][:, :], wxb[:, k * DI + mx * 128:k * DI + (mx + 1) * 128],
                                           actfm[:, k * T + tc * 512:k * T + (tc + 1) * 512],
                                           start=(k == 0), stop=(k == 7)) for k in range(8)][-1],
                              waits=[wxb_tok, pz_free[pb]] + ([xz_free[b]] if tc == 0 else []),
                              inc="sT")
                    e = S.op("vector",
                             lambda h, b=b, tc=tc, pb=pb: h.tensor_copy(
                                 xz[b][:, 1 + tc * 512:1 + (tc + 1) * 512], pz[pb][:, :]),
                             waits=[mt], inc="sV")
                    pz_free[pb] = e
                    ev.append(e)
                dg = None
                for k in range(4):
                    dg = S.op("vector",
                              lambda h, b=b, mx=mx, k=k: h.tensor_scalar(
                                  diag[b][:, k * 128:(k + 1) * 128], identb[:, :],
                                  cw_t[:, mx * 4 + k:mx * 4 + k + 1], None, ALU.mult),
                              waits=[dg_free[b]] if k == 0 else [], inc="sV")
                for tc in range(NTC):
                    pb = tc % 2
                    ct = S.op("tensor",
                              lambda h, b=b, tc=tc, pb=pb: [
                                  h.matmul(pc[pb][:, :], diag[b][:, k * 128:(k + 1) * 128],
                                           xz[b][:, tc * 512 + k: tc * 512 + k + 512],
                                           start=(k == 0), stop=(k == 3)) for k in range(4)][-1],
                              waits=[ev[-1], dg, zpad[b], pc_free[pb]], inc="sT")
                    if tc == NTC - 1:
                        xz_free[b] = ct
                        dg_free[b] = ct
                    u_done[(mx, tc)] = S.op("scalar",
                                            lambda h, mx=mx, tc=tc, pb=pb: h.activation(
                                                big[:, mx * T + tc * 512: mx * T + (tc + 1) * 512],
                                                pc[pb][:, :], AF.Silu, bias=cb_t[:, mx:mx + 1]),
                                            waits=[ct], inc="sA")
                    pc_free[pb] = u_done[(mx, tc)]
                u_spill[mx] = S.op("scalar",
                                   lambda h, mx=mx: h.dma_start(
                                       out=u_d[mx * 128:(mx + 1) * 128, :],
                                       in_=big[:, mx * T:(mx + 1) * T]),
                                   waits=[u_done[(mx, tc)] for tc in range(NTC)],
                                   inc="dSp", incn=16)
            m1_done = S.snap()
        pW.close()

        with ExitStack() as p2:
            pec = p2.enter_context
            wdta = pec(nc.sbuf_tensor("wdta", [128, 8 * DI], BF16))
            wdtb = pec(nc.sbuf_tensor("wdtb", [128, 8 * DI], BF16))
            wx_t = pec(nc.sbuf_tensor("wx_t", [128, 16 * 32], BF16))
            bst = [pec(nc.sbuf_tensor(f"bst{i}", [64, 512], BF16)) for i in range(2)]
            tmp1 = pec(nc.sbuf_tensor("tmp1", [128, 512], F32))
            dtt = [pec(nc.sbuf_tensor(f"dtt{i}", [128, 512], F32)) for i in range(2)]
            pd = [pec(nc.psum_tensor(f"pd{i}", [128, 512], F32)) for i in range(2)]
            pbcB = pec(nc.psum_tensor("pbcB", [16, 512], F32))
            pbcC = pec(nc.psum_tensor("pbcC", [16, 512], F32))

            wdtA = S.op("gpsimd",
                        lambda h: h.dma_start(out=wdta[:, :].rearrange("p (k m) -> p k m", m=DI),
                                              in_=wdt_h[0:1024, :].rearrange("(k p) m -> p k m", p=128)),
                        waits=m1_done, inc="dW", incn=16)
            wdtB = S.op("gpsimd",
                        lambda h: h.dma_start(out=wdtb[:, :].rearrange("p (k m) -> p k m", m=DI),
                                              in_=wdt_h[1024:2048, :].rearrange("(k p) m -> p k m", p=128)),
                        inc="dW", incn=16)
            wxT = S.op("gpsimd",
                       lambda h: h.dma_start(out=wx_t[:, :].rearrange("p (k m) -> p k m", m=32),
                                             in_=wx_h[:, :].rearrange("(k p) m -> p k m", p=128)),
                       inc="dW", incn=16)

            pbcB_free = None
            pbcC_free = None
            bst_free = [None, None]
            for tc in range(NTC):
                sb = tc % 2
                mb = S.op("tensor",
                          lambda h, tc=tc: [h.matmul(pbcB[:, :], wx_t[:, k * 32:k * 32 + 16],
                                                     big[:, k * T + tc * 512:k * T + (tc + 1) * 512],
                                                     start=(k == 0), stop=(k == 15)) for k in range(16)][-1],
                          waits=[wxT, pbcB_free] + (m1_done if tc == 0 else []), inc="sT")
                eb = S.op("vector", lambda h, sb=sb: h.tensor_copy(bst[sb][0:16, :], pbcB[:, :]),
                          waits=[mb, bst_free[sb]], inc="sV")
                pbcB_free = eb
                mc = S.op("tensor",
                          lambda h, tc=tc: [h.matmul(pbcC[:, :], wx_t[:, k * 32 + 16:k * 32 + 32],
                                                     big[:, k * T + tc * 512:k * T + (tc + 1) * 512],
                                                     start=(k == 0), stop=(k == 15)) for k in range(16)][-1],
                          waits=[pbcC_free], inc="sT")
                ecv = S.op("vector", lambda h, sb=sb: h.tensor_copy(bst[sb][32:48, :], pbcC[:, :]),
                           waits=[mc, eb], inc="sV")
                pbcC_free = ecv
                bc_spill[tc] = S.op("scalar",
                                    lambda h, sb=sb, tc=tc: h.dma_start(
                                        out=bc_d[:, tc * 512:(tc + 1) * 512], in_=bst[sb][:, :]),
                                    waits=[ecv], inc="dSp", incn=16)
                bst_free[sb] = bc_spill[tc]

            pd_free = [None, None]
            dtt_free = [None, None]
            v1prev = [None]
            for md in range(16):
                for tc in range(NTC):
                    pb = (md * NTC + tc) % 2
                    mt = S.op("tensor",
                              lambda h, md=md, tc=tc, pb=pb: [
                                  h.matmul(pd[pb][:, :],
                                           (wdta if k < 8 else wdtb)[:, (k % 8) * DI + md * 128:(k % 8) * DI + (md + 1) * 128],
                                           big[:, k * T + tc * 512:k * T + (tc + 1) * 512],
                                           start=(k == 0), stop=(k == 15)) for k in range(16)][-1],
                              waits=[wdtA, wdtB, pd_free[pb]], inc="sT")
                    a1 = S.op("scalar",
                              lambda h, md=md, pb=pb: h.activation(tmp1[:, :], pd[pb][:, :], AF.Abs,
                                                                   bias=bdt_t[:, md:md + 1]),
                              waits=[mt, v1prev[0]], inc="sA")
                    a2 = S.op("scalar", lambda h: h.activation(tmp1[:, :], tmp1[:, :], AF.Exp, scale=-1.0),
                              inc="sA")
                    a3 = S.op("scalar", lambda h: h.activation(tmp1[:, :], tmp1[:, :], AF.Ln, bias=1.0),
                              inc="sA")
                    db = pb
                    a4 = S.op("scalar",
                              lambda h, md=md, pb=pb, db=db: h.activation(dtt[db][:, :], pd[pb][:, :], AF.Relu,
                                                                          bias=bdt_t[:, md:md + 1]),
                              waits=[dtt_free[db]], inc="sA")
                    pd_free[pb] = a4
                    v1 = S.op("vector",
                              lambda h, db=db: h.tensor_add(dtt[db][:, :], dtt[db][:, :], tmp1[:, :]),
                              waits=[a4, a3], inc="sV")
                    v1prev[0] = v1
                    ddma[(md, tc)] = S.op("scalar",
                                          lambda h, md=md, tc=tc, db=db: h.dma_start(
                                              out=delta_d[md * 128:(md + 1) * 128, tc * 512:(tc + 1) * 512],
                                              in_=dtt[db][:, :]),
                                          waits=[v1], inc="dSp", incn=16)
                    dtt_free[db] = ddma[(md, tc)]
            m2a_done = S.snap()
        pA.close()
        pa_done = m2a_done

        with ExitStack() as p3:
            pec = p3.enter_context
            ring = pec(nc.sbuf_tensor("ring", [128, 4 * 8192], BF16))
            Bbc = pec(nc.sbuf_tensor("Bbc", [128, 8192], BF16))
            Cbc = pec(nc.sbuf_tensor("Cbc", [128, 8192], BF16))
            brc = [pec(nc.sbuf_tensor(f"brc{i}", [16, 512], BF16)) for i in range(2)]
            crc = [pec(nc.sbuf_tensor(f"crc{i}", [16, 512], BF16)) for i in range(2)]
            bhe = pec(nc.sbuf_tensor("bhe", [128, 8192], BF16))
            oy = pec(nc.sbuf_tensor("oy", [128, 4096], F32))
            x1fm = pec(nc.sbuf_tensor("x1fm", [128, 8 * 512], BF16))
            sel16 = pec(nc.sbuf_tensor("sel16_s", [16, 16 * 128], BF16))
            sel8 = pec(nc.sbuf_tensor("sel8_s", [8, 8 * 128], F32))
            a_t = pec(nc.sbuf_tensor("a_t", [128, 16 * DS], F32))
            dv_t = pec(nc.sbuf_tensor("dv_t", [128, 16], F32))
            hc = pec(nc.sbuf_tensor("hc", [128, 256], F32))
            dtl = [pec(nc.sbuf_tensor(f"dtl{i}", [128, 512], F32)) for i in range(2)]
            dAt = [pec(nc.sbuf_tensor(f"dAt{i}", [128, 512], F32)) for i in range(2)]
            yac = [pec(nc.sbuf_tensor(f"yac{i}", [128, 512], F32)) for i in range(2)]
            dut = [pec(nc.sbuf_tensor(f"dut{i}", [128, 512], BF16)) for i in range(2)]
            dBt = [pec(nc.sbuf_tensor(f"dBt{i}", [128, 512], BF16)) for i in range(2)]
            ht = [pec(nc.sbuf_tensor(f"ht{i}", [128, 512], BF16)) for i in range(2)]
            pt = [pec(nc.sbuf_tensor(f"pt{i}", [128, 512], BF16)) for i in range(2)]
            szs = [pec(nc.sbuf_tensor(f"szs{i}", [128, 512], BF16)) for i in range(2)]
            ybt = [pec(nc.sbuf_tensor(f"ybt{i}", [128, 512], BF16)) for i in range(2)]
            ut = [pec(nc.sbuf_tensor(f"ut{i}", [128, 512], BF16)) for i in range(2)]
            gwt_t = pec(nc.sbuf_tensor("gwt_s", [128, 8 * E], BF16))
            b1_t = pec(nc.sbuf_tensor("b1_s", [128, 8 * E], F32))
            b3_t = pec(nc.sbuf_tensor("b3_s", [128, 8 * E], F32))
            bs1_t = pec(nc.sbuf_tensor("bs1_s", [128, 16], F32))
            bs3_t = pec(nc.sbuf_tensor("bs3_s", [128, 16], F32))
            bs2_t = pec(nc.sbuf_tensor("bs2_s", [128, 8], F32))
            b2_t = pec(nc.sbuf_tensor("b2_s", [8, D], F32))
            gbbc = pec(nc.sbuf_tensor("gbbc", [128, E], F32))
            gT = pec(nc.sbuf_tensor("gT_s", [8, 512], F32))
            Gts = pec(nc.sbuf_tensor("Gts", [128, 8 * 512], BF16))
            xrm = pec(nc.sbuf_tensor("xrm", [128, D], F32))
            otm = pec(nc.sbuf_tensor("otm", [128, D], F32))
            silu_t = [pec(nc.sbuf_tensor(f"silu{i}", [128, 512], BF16)) for i in range(2)]
            lg = pec(nc.sbuf_tensor("lg", [128, E], F32))
            sc_t = pec(nc.sbuf_tensor("sc_t", [128, E], F32))
            sc2_t = pec(nc.sbuf_tensor("sc2_t", [128, E], F32))
            mk = pec(nc.sbuf_tensor("mk", [128, E], F32))
            gtm = pec(nc.sbuf_tensor("gtm", [128, E], F32))
            c_ns = pec(nc.sbuf_tensor("c_ns", [128, 1], F32))
            c_mean = pec(nc.sbuf_tensor("c_mean", [128, 1], F32))
            c_sq = pec(nc.sbuf_tensor("c_sq", [128, 1], F32))
            c_var = pec(nc.sbuf_tensor("c_var", [128, 1], F32))
            c_rstd = pec(nc.sbuf_tensor("c_rstd", [128, 1], F32))
            ph1 = [pec(nc.psum_tensor(f"ph1_{i}", [128, 512], F32)) for i in range(2)]
            ph3 = [pec(nc.psum_tensor(f"ph3_{i}", [128, 512], F32)) for i in range(2)]
            py = [pec(nc.psum_tensor(f"py{i}", [128, 512], F32)) for i in range(2)]
            pf = [pec(nc.psum_tensor(f"pf{i}", [128, 128], F32)) for i in range(2)]

            L = {}
            bct = {}

            c0 = cl(sel16[:, :], sel16_h[:, :], waits=pa_done)
            cl(sel8[:, :], sel8_h[:, :])
            tok_at = cl(a_t[:, :], a_h[:, :])
            tok_dvt = cl(dv_t[:, :], dv_h[:, :])
            ffc = []
            ffc.append(cl(gwt_t[:, :], gwt_h[:, :]))
            ffc.append(cl(b1_t[:, :], b1t_h[:, :]))
            ffc.append(cl(b3_t[:, :], b3t_h[:, :]))
            ffc.append(cl(bs1_t[:, :], bs1_h[:, :]))
            ffc.append(cl(bs3_t[:, :], bs3_h[:, :]))
            ffc.append(cl(bs2_t[:, :], bs2_h[:, :]))
            ffc.append(cl(b2_t[:, :], b2_h[:, :]))
            ffc_done = ffc[-1]
            tok_hc0 = S.op("vector", lambda h: h.memset(hc[:, :], 0.0),
                           waits=pa_done, inc="sV")
            gb1 = S.op("tensor", lambda h: h.matmul(pf[0][:, 0:E], ones1[:, :], gbrow[:, :],
                                                    start=True, stop=True),
                       waits=pa_done, inc="sT")
            gb2 = S.op("vector", lambda h: h.tensor_copy(gbbc[:, :], pf[0][:, 0:E]),
                       waits=[gb1, tok_hc0], inc="sV")
            L[("pf", 0)] = gb2

            ring_free = [pa_done, pa_done, pa_done, pa_done]
            ring_ctr = [0]
            ring_seq = []
            ring_toks = []
            ring_cur = [0]

            def ring_pump():
                while len(ring_toks) < len(ring_seq) and len(ring_toks) - ring_cur[0] < 4:
                    sl = ring_ctr[0] % 4
                    ring_ctr[0] += 1
                    dram_ap = ring_seq[len(ring_toks)]
                    w = ring_free[sl]
                    ws = list(w) if isinstance(w, list) else [w]
                    tok = S.op("gpsimd",
                               lambda h, sl=sl, dram_ap=dram_ap: h.dma_start(
                                   out=ring[:, sl * 8192:(sl + 1) * 8192].rearrange("p (k m) -> p k m", m=1024),
                                   in_=dram_ap.rearrange("(k p) m -> p k m", p=128)),
                               waits=ws, inc="dW", incn=16)
                    ring_toks.append((sl, tok))

            def ring_get():
                sl, tok = ring_toks[ring_cur[0]]
                ring_cur[0] += 1
                return sl, tok

            def ring_consumed(sl, tok):
                ring_free[sl] = tok
                ring_pump()

            ybf_spill = {}
            x1_spill = {}
            ln1_done = {}
            lastB = {}
            lastC = {}

            def emit_bcast(c):
                cb2 = c % 2
                ws0 = [bc_spill[c], L.get(("brc", cb2))] + (pa_done if c < 2 else [])
                brl = S.op("sync",
                           lambda h, c=c, cb2=cb2: h.dma_start(
                               out=brc[cb2][:, :], in_=bc_d[0:16, c * 512:(c + 1) * 512]),
                           waits=ws0, inc="dX", incn=16, lanes=(0, 1, 2, 3, 4))
                crl = S.op("sync",
                           lambda h, c=c, cb2=cb2: h.dma_start(
                               out=crc[cb2][:, :], in_=bc_d[32:48, c * 512:(c + 1) * 512]),
                           waits=ws0, inc="dX", incn=16, lanes=(0, 1, 2, 3, 4))
                mmlast = None
                for s in range(16):
                    jp = s % 2
                    mmB = S.op("tensor",
                               lambda h, s=s, cb2=cb2, jp=jp: h.matmul(
                                   ph3[jp][:, :], sel16[:, s * 128:(s + 1) * 128],
                                   brc[cb2][0:16, :], start=True, stop=True),
                               waits=[brl, c0, L.get(("ph3", jp))], inc="sT")
                    evB = S.op("scalar",
                               lambda h, s=s, jp=jp: h.activation(Bbc[:, s * 512:(s + 1) * 512],
                                                                  ph3[jp][:, :], AF.Identity),
                               waits=[mmB, lastB.get((c - 1, s))] + (pa_done if c == 0 and s == 0 else []),
                               inc="sA")
                    L[("ph3", jp)] = evB
                    bct[(c, 'B', s)] = evB
                    mmC = S.op("tensor",
                               lambda h, s=s, cb2=cb2, jp=jp: h.matmul(
                                   ph1[jp][:, :], sel16[:, s * 128:(s + 1) * 128],
                                   crc[cb2][:, :], start=True, stop=True),
                               waits=[crl, c0, L.get(("ph1", jp))], inc="sT")
                    evC = S.op("scalar",
                               lambda h, s=s, jp=jp: h.activation(Cbc[:, s * 512:(s + 1) * 512],
                                                                  ph1[jp][:, :], AF.Identity),
                               waits=[mmC, lastC.get((c - 1, s))], inc="sA")
                    L[("ph1", jp)] = evC
                    bct[(c, 'C', s)] = evC
                    mmlast = mmC
                L[("brc", cb2)] = mmlast

            sfree = {}
            hcq = {}

            def emit_scan_iter(c, mx):
                db = mx % 2
                pa0 = pa_done if c == 0 else []
                dld = S.op("sync",
                           lambda h, mx=mx, c=c, db=db: h.dma_start(
                               out=dtl[db][:, :], in_=delta_d[mx * 128:(mx + 1) * 128, c * 512:(c + 1) * 512]),
                           waits=[ddma[(mx, c)], sfree.get(("dt", db))] + pa0, inc="dX", incn=16, lanes=(0, 1, 2, 3, 4))
                uld = S.op("sync",
                           lambda h, mx=mx, c=c, db=db: h.dma_start(
                               out=ut[db][:, :], in_=u_d[mx * 128:(mx + 1) * 128, c * 512:(c + 1) * 512]),
                           waits=[u_spill[mx], sfree.get(("ut", db))] + pa0, inc="dX", incn=16, lanes=(0, 1, 2, 3, 4))
                sld = S.op("sync",
                           lambda h, mx=mx, c=c, db=db: h.dma_start(
                               out=szs[db][:, :], in_=sz_d[mx * 128:(mx + 1) * 128, c * 512:(c + 1) * 512]),
                           waits=[sz_spill[(mx, c)], sfree.get(("sz", db))] + pa0, inc="dX", incn=16, lanes=(0, 1, 2, 3, 4))
                dutk = S.op("vector",
                            lambda h, db=db: h.tensor_tensor(dut[db][:, :], dtl[db][:, :], ut[db][:, :],
                                                             ALU.mult),
                            waits=[dld, uld, sfree.get(("dut", db)), tok_hc0], inc="sV")
                acc = None
                for s in range(16):
                    j = s % 2
                    adA = S.op("scalar",
                               lambda h, mx=mx, s=s, j=j, db=db: h.activation(
                                   dAt[j][:, :], dtl[db][:, :], AF.Exp,
                                   scale=a_t[:, mx * DS + s:mx * DS + s + 1]),
                               waits=[dld, sfree.get(("dA", j)), tok_at], inc="sA")
                    dBm = S.op("vector",
                               lambda h, s=s, j=j, db=db: h.tensor_tensor(
                                   dBt[j][:, :], dut[db][:, :], Bbc[:, s * 512:(s + 1) * 512], ALU.mult),
                               waits=[dutk, sfree.get(("dB", j)), bct[(c, 'B', s)]], inc="sV")
                    if mx == 15:
                        lastB[(c, s)] = dBm
                    hsc = S.op("vector",
                               lambda h, mx=mx, s=s, j=j: h.tensor_tensor_scan(
                                   ht[j][:, :], dAt[j][:, :], dBt[j][:, :],
                                   hc[:, mx * DS + s:mx * DS + s + 1], ALU.mult, ALU.add),
                               waits=[adA, dBm, sfree.get(("h", j)), sfree.get(("h2", j)),
                                      hcq.get((mx, s))], inc="sV")
                    sfree[("dA", j)] = hsc
                    sfree[("dB", j)] = hsc
                    hcw = S.op("scalar",
                               lambda h, mx=mx, s=s, j=j: h.activation(
                                   hc[:, mx * DS + s:mx * DS + s + 1], ht[j][:, 511:512], AF.Identity),
                               waits=[hsc], inc="sA")
                    hcq[(mx, s)] = hcw
                    sfree[("h2", j)] = hcw
                    pm = S.op("vector",
                              lambda h, s=s, j=j: h.tensor_tensor(
                                  pt[j][:, :], ht[j][:, :], Cbc[:, s * 512:(s + 1) * 512], ALU.mult),
                              waits=[hsc, bct[(c, 'C', s)], sfree.get(("pt", j))], inc="sV")
                    sfree[("h", j)] = pm
                    if mx == 15:
                        lastC[(c, s)] = pm
                    if s == 0:
                        acc = S.op("vector",
                                   lambda h, j=j, db=db: h.tensor_copy(yac[db][:, :], pt[j][:, :]),
                                   waits=[pm, sfree.get(("yac", db))], inc="sV")
                    else:
                        acc = S.op("gpsimd",
                                   lambda h, j=j, db=db: h.tensor_tensor(
                                       yac[db][:, :], yac[db][:, :], pt[j][:, :], ALU.add),
                                   waits=[pm, acc], inc="sP")
                    sfree[("pt", j)] = acc
                y1 = S.op("vector",
                          lambda h, mx=mx, db=db: h.scalar_tensor_tensor(
                              yac[db][:, :], ut[db][:, :], dv_t[:, mx:mx + 1], yac[db][:, :],
                              ALU.mult, ALU.add),
                          waits=[acc, uld, tok_dvt], inc="sV")
                sfree[("ut", db)] = y1
                yo = mx % 2
                y2 = S.op("gpsimd",
                          lambda h, yo=yo, db=db: h.tensor_tensor(
                              ybt[yo][:, :], yac[db][:, :], szs[db][:, :], ALU.mult),
                          waits=[y1, sld, sfree.get(("ybt", yo))], inc="sP")
                sfree[("yac", db)] = y2
                sfree[("sz", db)] = y2
                sfree[("dt", db)] = y2
                sfree[("dut", db)] = y2
                ydma = S.op("scalar",
                            lambda h, mx=mx, c=c, yo=yo: h.dma_start(
                                out=ybf_d[mx * 128:(mx + 1) * 128, c * 512:(c + 1) * 512],
                                in_=ybt[yo][:, :]),
                            waits=[y2], inc="dSp", incn=16)
                sfree[("ybt", yo)] = ydma
                ybf_spill[(mx, c)] = ydma

            def emit_ln(r, src, coef, res_spill, do_x1, out_tag):
                rl = r % 4
                xsrc = x_h if out_tag == "ln1" else x1_d
                xw = [sfree.get("xrm")]
                if out_tag == "ln2":
                    xw.append(x1_spill[r])
                xr = S.op("sync",
                          lambda h, r=r, xsrc=xsrc: h.dma_start(out=xrm[:, :],
                                                                in_=xsrc[r * 128:(r + 1) * 128, :]),
                          waits=xw, inc="dX", incn=16, lanes=(0, 1, 2, 3, 4))
                ot = None
                for dc in range(8):
                    fb = dc % 2
                    tt = S.op("tensor",
                              lambda h, dc=dc, rl=rl: h.transpose(
                                  pf[dc % 2][:, 0:128],
                                  oy[:, dc * 512 + rl * 128: dc * 512 + (rl + 1) * 128],
                                  identf[:, :]),
                              waits=[L.get(("oyw", dc)), L.get(("pf", fb))], inc="sT")
                    L[("oyrd", out_tag, dc)] = tt
                    ot = S.op("vector",
                              lambda h, dc=dc, fb=fb, coef=coef: h.scalar_tensor_tensor(
                                  otm[:, dc * 128:(dc + 1) * 128], xrm[:, dc * 128:(dc + 1) * 128],
                                  coef, pf[fb][:, 0:128], ALU.mult, ALU.add),
                              waits=[tt, xr, L.get("otm")], inc="sV")
                    L[("pf", fb)] = ot
                s1 = S.op("vector", lambda h: h.reduce_sum(c_ns[:, :], otm[:, :], axis=AX.X, negate=True),
                          waits=[ot], inc="sV")
                s2 = S.op("vector", lambda h: h.tensor_single_scalar(c_mean[:, :], c_ns[:, :], -1.0 / D, ALU.mult),
                          waits=[s1], inc="sV")
                s3 = S.op("scalar", lambda h: h.activation(xrm[:, :], otm[:, :], AF.Square,
                                                           accum_out=c_sq[:, 0:1]),
                          waits=[ot], inc="sA")
                sfree["xrm"] = s3
                s4 = S.op("scalar", lambda h: h.activation(c_var[:, :], c_mean[:, :], AF.Square),
                          waits=[s2], inc="sA")
                s5 = S.op("vector", lambda h: h.tensor_scalar(c_ns[:, :], c_sq[:, :], 1.0 / D,
                                                              c_var[:, 0:1], ALU.mult, ALU.subtract),
                          waits=[s3, s4], inc="sV")
                s6 = S.op("scalar", lambda h: h.activation(c_rstd[:, :], c_ns[:, :], AF.Ln, bias=fc[:, 0:1]),
                          waits=[s5], inc="sA")
                s7 = S.op("scalar", lambda h: h.activation(c_rstd[:, :], c_rstd[:, :], AF.Exp, scale=fc[:, 1:2]),
                          inc="sA")
                ap = S.op("vector",
                          lambda h: h.tensor_scalar(otm[:, :], otm[:, :], c_mean[:, 0:1],
                                                    c_rstd[:, 0:1], ALU.subtract, ALU.mult),
                          waits=[s7, s3], inc="sV")
                if out_tag == "ln1":
                    sp = S.op("sync",
                              lambda h, r=r: h.dma_start(out=x1_d[r * 128:(r + 1) * 128, :], in_=otm[:, :]),
                              waits=[ap], inc="dX", incn=16, lanes=(0, 1, 2, 3, 4))
                    x1_spill[r] = sp
                    lastev = None
                    for dc in range(8):
                        bb = dc % 2
                        tb = S.op("tensor",
                                  lambda h, dc=dc, bb=bb: h.transpose(
                                      pf[bb][:, :], otm[:, dc * 128:(dc + 1) * 128], identf[:, :]),
                                  waits=[ap, L.get(("pf", bb))], inc="sT")
                        lastev = S.op("scalar",
                                      lambda h, dc=dc, rl=rl, bb=bb: h.activation(
                                          x1fm[:, dc * 512 + rl * 128:dc * 512 + (rl + 1) * 128],
                                          pf[bb][:, :], AF.Identity),
                                      waits=[tb, L.get("x1fm")], inc="sA")
                        L[("pf", bb)] = lastev
                    L["otm"] = lastev
                    ln1_done[r] = lastev
                else:
                    L["otm"] = ap
                    L["outdma"] = S.op("sync",
                                       lambda h, r=r: h.dma_start(out=out_h[r * 128:(r + 1) * 128, :],
                                                                  in_=otm[:, :]),
                                       waits=[ap], inc="dO", incn=16, lanes=(0, 1, 2, 3, 4))

            def gen(c):
                if c > 0:
                    for rl in range(4):
                        emit_ln((c - 1) * 4 + rl, None, 1.0, None, False, "ln2")
                        yield "ln2"
                if c + 1 < NCH:
                    emit_bcast(c + 1)
                yield "bcast"
                experts = []
                for e in range(E):
                    experts.append(dict(
                        w1=w1t_h[e * D:(e + 1) * D, :], w3=w3t_h[e * D:(e + 1) * D, :],
                        w2=w2t_h[e * IN:(e + 1) * IN, :],
                        b1c=[b1_t[:, e * 8 + mi:e * 8 + mi + 1] for mi in range(8)],
                        b3c=[b3_t[:, e * 8 + mi:e * 8 + mi + 1] for mi in range(8)],
                        gated=e))
                experts.append(dict(
                    w1=ws1t_h[:, 0:1024], w3=ws3t_h[:, 0:1024], w2=ws2t_h[0:1024, :],
                    b1c=[bs1_t[:, mi:mi + 1] for mi in range(8)],
                    b3c=[bs3_t[:, mi:mi + 1] for mi in range(8)], gated=None))
                experts.append(dict(
                    w1=ws1t_h[:, 1024:2048], w3=ws3t_h[:, 1024:2048], w2=ws2t_h[1024:2048, :],
                    b1c=[bs1_t[:, 8 + mi:8 + mi + 1] for mi in range(8)],
                    b3c=[bs3_t[:, 8 + mi:8 + mi + 1] for mi in range(8)], gated=None))
                ring_seq.append(wout_h[0:1024, :])
                ring_seq.append(wout_h[1024:2048, :])
                for p in range(5):
                    e0x, e1x = experts[2 * p], experts[2 * p + 1]
                    ring_seq.extend([e0x["w1"], e0x["w3"], e1x["w1"], e1x["w3"],
                                     e0x["w2"], e1x["w2"]])
                ring_pump()
                wA_sl, wA = ring_get()
                wB_sl, wB = ring_get()
                ybl = []
                for k in range(16):
                    ybl.append(S.op("sync",
                                    lambda h, k=k, c=c: h.dma_start(
                                        out=bhe[:, k * 512:(k + 1) * 512],
                                        in_=ybf_d[k * 128:(k + 1) * 128, c * 512:(c + 1) * 512]),
                                    waits=[ybf_spill[(k, c)], L.get(("bhe", k))] +
                                          (pa_done if c == 0 else []),
                                    inc="dX", incn=16, lanes=(0, 1, 2, 3, 4)))
                yield "m3"
                mts = []
                for mo in range(8):
                    pbk = mo % 2
                    mt = S.op("tensor",
                              lambda h, mo=mo, pbk=pbk, wA_sl=wA_sl, wB_sl=wB_sl: [
                                  h.matmul(py[pbk][:, :],
                                           ring[:, (wA_sl if k < 8 else wB_sl) * 8192 +
                                                (k % 8) * 1024 + mo * 128:
                                                (wA_sl if k < 8 else wB_sl) * 8192 +
                                                (k % 8) * 1024 + (mo + 1) * 128],
                                           bhe[:, k * 512:(k + 1) * 512],
                                           start=(k == 0), stop=(k == 15)) for k in range(16)][-1],
                              waits=[wA, wB, L.get(("py", pbk))] + ybl, inc="sT")
                    mts.append(mt)
                    ev = S.op("scalar",
                              lambda h, mo=mo, pbk=pbk: h.activation(
                                  oy[:, mo * 512:(mo + 1) * 512], py[pbk][:, :], AF.Identity),
                              waits=[mt, L.get(("oyrd", "ln2", mo))], inc="sA")
                    L[("py", pbk)] = ev
                    L[("oyw", mo)] = ev
                ring_consumed(wA_sl, mts[-1])
                ring_consumed(wB_sl, mts[-1])
                for k in range(16):
                    L[("bhe", k)] = mts[-1]
                yield "m3b"
                for rl in range(4):
                    emit_ln(c * 4 + rl, None, 2.0, None, True, "ln1")
                    yield "ln1"

                gate_toks = []
                for rl in range(4):
                    r = c * 4 + rl
                    gm = S.op("tensor",
                              lambda h, rl=rl: [
                                  h.matmul(pf[0][:, 0:E], x1fm[:, k * 512 + rl * 128:k * 512 + (rl + 1) * 128],
                                           gwt_t[:, k * E:(k + 1) * E],
                                           start=(k == 0), stop=(k == 7)) for k in range(8)][-1],
                              waits=[ln1_done[r], gb2, ffc_done, L.get(("pf", 0))], inc="sT")
                    g1 = S.op("vector", lambda h: h.tensor_add(lg[:, :], pf[0][:, 0:E], gbbc[:, :]),
                              waits=[gm, L.get("lg")], inc="sV")
                    L[("pf", 0)] = g1
                    g2 = S.op("vector", lambda h: h.reduce_max(c_ns[:, :], lg[:, :], axis=AX.X),
                              waits=[g1], inc="sV")
                    g2b = S.op("vector", lambda h: h.tensor_single_scalar(c_mean[:, :], c_ns[:, :], -1.0, ALU.mult),
                               waits=[g2], inc="sV")
                    g3 = S.op("scalar", lambda h: h.activation(sc_t[:, :], lg[:, :], AF.Exp, bias=c_mean[:, 0:1]),
                              waits=[g2b, L.get("sc_t")], inc="sA")
                    g4 = S.op("vector", lambda h: h.reduce_sum(c_sq[:, :], sc_t[:, :], axis=AX.X),
                              waits=[g3], inc="sV")
                    g5 = S.op("vector", lambda h: h.reciprocal(c_var[:, :], c_sq[:, :]), waits=[g4], inc="sV")
                    g6 = S.op("vector", lambda h: h.tensor_single_scalar(sc_t[:, :], sc_t[:, :],
                                                                         c_var[:, 0:1], ALU.mult), waits=[g5], inc="sV")
                    g7 = S.op("vector", lambda h: h.reduce_max(c_ns[:, :], sc_t[:, :], axis=AX.X),
                              waits=[g6], inc="sV")
                    g8 = S.op("vector", lambda h: h.tensor_single_scalar(mk[:, :], sc_t[:, :], c_ns[:, 0:1],
                                                                         ALU.is_ge), waits=[g7], inc="sV")
                    g9 = S.op("vector", lambda h: h.scalar_tensor_tensor(sc2_t[:, :], mk[:, :], -1e30, sc_t[:, :],
                                                                         ALU.mult, ALU.add), waits=[g8], inc="sV")
                    g10 = S.op("vector", lambda h: h.reduce_max(c_ns[:, :], sc2_t[:, :], axis=AX.X),
                               waits=[g9], inc="sV")
                    g11 = S.op("vector", lambda h: h.tensor_single_scalar(mk[:, :], sc_t[:, :], c_ns[:, 0:1],
                                                                          ALU.is_ge), waits=[g10], inc="sV")
                    g12 = S.op("vector", lambda h: h.tensor_mul(gtm[:, :], sc_t[:, :], mk[:, :]),
                               waits=[g11], inc="sV")
                    L["lg"] = g12
                    L["sc_t"] = g12
                    tg = S.op("tensor", lambda h: h.transpose(pf[1][0:8, 0:128], gtm[:, :], identf[:, :]),
                              waits=[g12, L.get(("pf", 1))], inc="sT")
                    gev = S.op("vector",
                               lambda h, rl=rl: h.tensor_copy(gT[0:8, rl * 128:(rl + 1) * 128],
                                                              pf[1][0:8, 0:128]),
                               waits=[tg, L.get("gtm")], inc="sV")
                    L[("pf", 1)] = gev
                    L["gtm"] = gev
                    gate_toks.append(gev)
                gate_done = gate_toks[-1]
                gts_tok = {}
                for e in range(E):
                    gmm = S.op("tensor",
                               lambda h, e=e: h.matmul(
                                   py[0][:, :], sel8[:, e * 128:(e + 1) * 128],
                                   gT[:, :], start=True, stop=True),
                               waits=[gate_done, ffc_done, L.get(("py", 0))], inc="sT")
                    gts_tok[e] = S.op("scalar",
                                      lambda h, e=e: h.activation(
                                          Gts[:, e * 512:(e + 1) * 512], py[0][:, :], AF.Identity),
                                      waits=[gmm, L.get(("gts", e))], inc="sA")
                    L[("py", 0)] = gts_tok[e]
                yield "gates"

                last_x1_rd = None
                for p in range(5):
                    e0, e1 = experts[2 * p], experts[2 * p + 1]
                    sl1a, t1a = ring_get()
                    sl3a, t3a = ring_get()
                    hlast = {}
                    for half, ex in ((0, e0), (1, e1)):
                        if half == 0:
                            sl1, t1 = sl1a, t1a
                            sl3, t3 = sl3a, t3a
                        else:
                            sl1, t1 = ring_get()
                            sl3, t3 = ring_get()
                        gated = ex["gated"]
                        m1last = None
                        m3last = None
                        for mi in range(8):
                            jp = mi % 2
                            m1_ = S.op("tensor",
                                       lambda h, sl1=sl1, mi=mi, jp=jp: [
                                           h.matmul(ph1[jp][:, :],
                                                    ring[:, sl1 * 8192 + k * 1024 + mi * 128:
                                                         sl1 * 8192 + k * 1024 + (mi + 1) * 128],
                                                    x1fm[:, k * 512:(k + 1) * 512],
                                                    start=(k == 0), stop=(k == 7)) for k in range(8)][-1],
                                       waits=[t1, L.get(("ph1", jp))], inc="sT")
                            m1last = m1_
                            sb = mi % 2
                            sa = S.op("scalar",
                                      lambda h, mi=mi, sb=sb, ex=ex, jp=jp: h.activation(
                                          silu_t[sb][:, :], ph1[jp][:, :], AF.Silu, bias=ex["b1c"][mi]),
                                      waits=[m1_, L.get(("silu", sb))], inc="sA")
                            L[("ph1", jp)] = sa
                            m3_ = S.op("tensor",
                                       lambda h, sl3=sl3, mi=mi, jp=jp: [
                                           h.matmul(ph3[jp][:, :],
                                                    ring[:, sl3 * 8192 + k * 1024 + mi * 128:
                                                         sl3 * 8192 + k * 1024 + (mi + 1) * 128],
                                                    x1fm[:, k * 512:(k + 1) * 512],
                                                    start=(k == 0), stop=(k == 7)) for k in range(8)][-1],
                                       waits=[t3, L.get(("ph3", jp))], inc="sT")
                            m3last = m3_
                            hv = S.op("vector",
                                      lambda h, mi=mi, half=half, ex=ex, sb=sb, jp=jp: h.scalar_tensor_tensor(
                                          bhe[:, half * 4096 + mi * 512:half * 4096 + (mi + 1) * 512],
                                          ph3[jp][:, :], ex["b3c"][mi], silu_t[sb][:, :], ALU.add, ALU.mult),
                                      waits=[m3_, sa, L.get(("bhe", 8 * half + mi))], inc="sV")
                            L[("ph3", jp)] = hv
                            L[("silu", sb)] = hv
                            if gated is not None:
                                hv = S.op("gpsimd",
                                          lambda h, mi=mi, half=half, gated=gated: h.tensor_tensor(
                                              bhe[:, half * 4096 + mi * 512:half * 4096 + (mi + 1) * 512],
                                              bhe[:, half * 4096 + mi * 512:half * 4096 + (mi + 1) * 512],
                                              Gts[:, gated * 512:(gated + 1) * 512], ALU.mult),
                                          waits=[hv, gts_tok[gated]], inc="sP")
                            hlast[8 * half + mi] = hv
                        if gated is not None:
                            L[("gts", gated)] = hlast[8 * half + 7]
                        ring_consumed(sl1, m1last)
                        ring_consumed(sl3, m3last)
                        last_x1_rd = m3last
                        yield "he"
                    sl2a, t2a = ring_get()
                    sl2b, t2b = ring_get()
                    wmts = []
                    for mo in range(8):
                        pbk = mo % 2

                        def w2mm(h, sl2a=sl2a, sl2b=sl2b, mo=mo, pbk=pbk, p=p):
                            last = None
                            if p == 0:
                                last = h.matmul(py[pbk][:, :], b2_t[:, mo * 128:(mo + 1) * 128],
                                                gT[:, :], start=True, stop=False)
                            for k in range(16):
                                sl = sl2a if k < 8 else sl2b
                                last = h.matmul(py[pbk][:, :],
                                                ring[:, sl * 8192 + (k % 8) * 1024 + mo * 128:
                                                     sl * 8192 + (k % 8) * 1024 + (mo + 1) * 128],
                                                bhe[:, k * 512:(k + 1) * 512],
                                                start=(p != 0 and k == 0), stop=(k == 15))
                            return last
                        mt = S.op("tensor", w2mm,
                                  waits=[t2a, t2b, L.get(("py", pbk)), gate_done] +
                                        [hlast[k] for k in range(16)], inc="sT")
                        wmts.append(mt)
                        if p == 0:
                            ev = S.op("vector",
                                      lambda h, mo=mo, pbk=pbk: h.tensor_scalar(
                                          oy[:, mo * 512:(mo + 1) * 512], py[pbk][:, :],
                                          bs2_t[:, mo:mo + 1], None, ALU.add),
                                      waits=[mt, L.get(("oyrd", "ln1", mo)), L.get(("oyw2", mo))], inc="sV")
                        else:
                            ev = S.op("vector",
                                      lambda h, mo=mo, pbk=pbk: h.tensor_tensor(
                                          oy[:, mo * 512:(mo + 1) * 512],
                                          oy[:, mo * 512:(mo + 1) * 512], py[pbk][:, :], ALU.add),
                                      waits=[mt, L.get(("oyw2", mo))], inc="sV")
                        L[("py", pbk)] = ev
                        L[("oyw2", mo)] = ev
                        if p == 4:
                            L[("oyw", mo)] = ev
                    ring_consumed(sl2a, wmts[-1])
                    ring_consumed(sl2b, wmts[-1])
                    for k in range(16):
                        L[("bhe", k)] = wmts[-1]
                    yield "pair"
                L["x1fm"] = last_x1_rd

            def pump_until(g, stop_tags):
                while True:
                    try:
                        tag = next(g)
                    except StopIteration:
                        return None
                    if tag in stop_tags:
                        return tag

            emit_bcast(0)
            g_prev = None
            for c in range(NCH):
                if g_prev is not None:
                    if pump_until(g_prev, ("bcast",)) is None:
                        g_prev = None
                for mx in range(16):
                    emit_scan_iter(c, mx)
                    if g_prev is not None:
                        if mx == 0:
                            if pump_until(g_prev, ("gates",)) is None:
                                g_prev = None
                        else:
                            try:
                                next(g_prev)
                            except StopIteration:
                                g_prev = None
                if g_prev is not None:
                    for _ in g_prev:
                        pass
                g_prev = gen(c)
            for _ in g_prev:
                pass
            for rl in range(4):
                emit_ln((NCH - 1) * 4 + rl, None, 1.0, None, False, "ln2")

            S.op("sync", lambda h: h.nop(), waits=S.group_waits("dO"))

        with nc.Block() as block:
            S.emit(block)

    return nc


def _host_layout(ins):
    f = np.float32
    bb = ml_dtypes.bfloat16

    def tile_col(v, nk):
        v = np.asarray(v, f).reshape(nk, 128, -1).transpose(1, 0, 2)
        return np.ascontiguousarray(v.reshape(128, -1))

    w = {}
    w["w_in"] = np.ascontiguousarray(np.asarray(ins["L0_attn_W_in"], f))
    w["cw"] = tile_col(np.asarray(ins["L0_attn_conv_w"], f).reshape(4, DI).T, 16)
    w["cb"] = tile_col(np.asarray(ins["L0_attn_conv_b"], f).reshape(DI, 1), 16)
    w["w_x"] = np.ascontiguousarray(np.asarray(ins["L0_attn_W_x"], f))
    w["w_dt"] = np.ascontiguousarray(np.asarray(ins["L0_attn_W_dt"], f))
    w["b_dt"] = tile_col(np.asarray(ins["L0_attn_b_dt"], f).reshape(DI, 1), 16)
    w["w_out"] = np.ascontiguousarray(np.asarray(ins["L0_attn_W_out"], f))
    w["a_mat"] = tile_col(np.asarray(ins["L0_attn_A"], f), 16)
    w["d_vec"] = tile_col(np.asarray(ins["L0_attn_D"], f).reshape(DI, 1), 16)
    w["gate_wt"] = tile_col(np.ascontiguousarray(np.asarray(ins["L0_ffn_gate_w"], f).T), 8).astype(bb)
    w["gate_b"] = np.asarray(ins["L0_ffn_gate_b"], f).reshape(1, E)
    w["w1t"] = np.ascontiguousarray(np.asarray(ins["L0_ffn_w1"], f).transpose(0, 2, 1).reshape(E * D, IN))
    w["w3t"] = np.ascontiguousarray(np.asarray(ins["L0_ffn_w3"], f).transpose(0, 2, 1).reshape(E * D, IN))
    w["w2t"] = np.ascontiguousarray(np.asarray(ins["L0_ffn_w2"], f).transpose(0, 2, 1).reshape(E * IN, D))
    b1 = np.asarray(ins["L0_ffn_b1"], f)
    b3 = np.asarray(ins["L0_ffn_b3"], f)
    bt1 = np.zeros((128, 8 * E), f)
    bt3 = np.zeros((128, 8 * E), f)
    for e in range(E):
        for mi in range(8):
            bt1[:, e * 8 + mi] = b1[e, mi * 128:(mi + 1) * 128]
            bt3[:, e * 8 + mi] = b3[e, mi * 128:(mi + 1) * 128]
    w["b1t"] = bt1
    w["b3t"] = bt3
    w["b2"] = np.ascontiguousarray(np.asarray(ins["L0_ffn_b2"], f))
    w["ws1t"] = np.ascontiguousarray(np.asarray(ins["L0_ffn_ws1"], f).T)
    w["bs1"] = tile_col(np.asarray(ins["L0_ffn_bs1"], f).reshape(SH, 1), 16)
    w["ws3t"] = np.ascontiguousarray(np.asarray(ins["L0_ffn_ws3"], f).T)
    w["bs3"] = tile_col(np.asarray(ins["L0_ffn_bs3"], f).reshape(SH, 1), 16)
    w["ws2t"] = np.ascontiguousarray(np.asarray(ins["L0_ffn_ws2"], f).T)
    w["bs2"] = tile_col(np.asarray(ins["L0_ffn_bs2"], f).reshape(D, 1), 8)
    sel16 = np.zeros((16, 16 * 128), bb)
    for s in range(16):
        sel16[s, s * 128:(s + 1) * 128] = 1
    w["sel16"] = sel16
    sel8 = np.zeros((8, 8 * 128), np.float32)
    for s in range(8):
        sel8[s, s * 128:(s + 1) * 128] = 1
    w["sel8"] = sel8
    w["identb"] = np.eye(128, dtype=bb)
    w["identf"] = np.eye(128, dtype=f)
    w["ones1"] = np.ones((1, 128), f)
    fcv = np.zeros((128, 2), f)
    fcv[:, 0] = 1e-5
    fcv[:, 1] = -0.5
    w["fc"] = fcv
    return w


def kernel(**ins):
    T = ins["L0_attn_inp"].shape[1]
    w = _host_layout(ins)
    x = np.asarray(ins["L0_attn_inp"], np.float32)
    in_maps = []
    for b in range(NCORES):
        m = dict(w)
        m["x"] = np.ascontiguousarray(x[b])
        in_maps.append(m)
    nc = build_program(T)
    res = run_bass_kernel_spmd(nc, in_maps, core_ids=list(range(NCORES)))
    return np.stack([np.asarray(r["out"]) for r in res.results], axis=0)
